# version 33
# baseline (speedup 1.0000x reference)
import numpy as np
import ml_dtypes

import concourse.bass as bass
import concourse.mybir as mybir
from concourse.bass_utils import run_bass_kernel_spmd

F32 = mybir.dt.float32
BF16 = mybir.dt.bfloat16
AF = mybir.ActivationFunctionType
ALU = mybir.AluOpType
AX = mybir.AxisListType
PI = float(np.pi)

ENGINES = ("tensor", "scalar", "vector", "gpsimd", "sync")


class _Op:
    __slots__ = ("method", "args", "kwargs", "deps", "inc_pos", "waits", "note")

    def __init__(self, method, args, kwargs, note=""):
        self.method = method
        self.args = args
        self.kwargs = kwargs
        self.deps = []
        self.inc_pos = None
        self.waits = []
        self.note = note


class Prog:

    def __init__(self, nc):
        self.nc = nc
        self.ops = {e: [] for e in ENGINES}
        self.lastw = {}
        self.readers = {}
        self.pending = {}
        self.dma_slots = []
        self.dma_uses = []
        self.slot_for_key = {}
        self.eng_sems = {e: nc.alloc_semaphore(f"es_{e}") for e in ENGINES}
        self.slot_cons = {}

    def _slot_for(self, key):
        s = self.slot_for_key.get(key)
        if s is None:
            s = len(self.dma_slots)
            self.dma_slots.append(self.nc.alloc_semaphore(f"dsl_{key}"))
            self.dma_uses.append(0)
            self.slot_for_key[key] = s
        return s

    @staticmethod
    def _dep(deps, ref):
        if ref is not None and ref not in deps:
            deps.append(ref)

    def emit(self, eng, method, *args, reads=(), writes=(), dma=False, no_waw=False,
             note="", **kwargs):
        op = _Op(method, args, kwargs, note)
        deps = op.deps
        for k in reads:
            if k in self.pending:
                for r in self.pending.pop(k):
                    self._dep(deps, r)
            self._dep(deps, self.lastw.get(k))
        for k in writes:
            if k in self.pending:
                for r in self.pending.pop(k):
                    self._dep(deps, r)
            if not no_waw:
                self._dep(deps, self.lastw.get(k))
            for r in self.readers.get(k, ()):
                self._dep(deps, r)
        if eng == "tensor":
            op.deps = deps = [d for d in deps if not (d[0] == "E" and d[1] == "tensor")]
        data_deps = list(deps)
        if dma:
            slot = self._slot_for(writes[0])
            for r in self.slot_cons.get(slot, ()):
                self._dep(deps, r)
            self.slot_cons[slot] = []
            self.dma_uses[slot] += 1
            ref = ("D", slot, self.dma_uses[slot])
            op.inc_pos = ("D", slot)
        else:
            ref = ("E", eng, len(self.ops[eng]))
        self.ops[eng].append(op)
        for d in data_deps:
            if d[0] == "D":
                self.slot_cons.setdefault(d[1], []).append(ref)
        for k in reads:
            self.readers.setdefault(k, []).append(ref)
        for k in writes:
            self.lastw[k] = ref
            if not no_waw:
                self.readers[k] = []
        return ref

    def wait_only(self, eng, reads=(), note=""):
        self.emit(eng, None, reads=reads, note=note)

    def alias_barrier(self, old_keys, new_keys):
        refs = []
        for k in old_keys:
            w = self.lastw.get(k)
            if w is not None and w not in refs:
                refs.append(w)
            for r in self.readers.get(k, ()):
                if r not in refs:
                    refs.append(r)
        for k in new_keys:
            cur = self.pending.setdefault(k, [])
            for r in refs:
                if r not in cur:
                    cur.append(r)

    def finalize(self):
        needed = {e: set() for e in ENGINES}
        for eng in ENGINES:
            frontier = {}
            for op in self.ops[eng]:
                dmax = {}
                for ref in op.deps:
                    key = (ref[0], ref[1])
                    if dmax.get(key, -1) < ref[2]:
                        dmax[key] = ref[2]
                w = []
                for key, n in dmax.items():
                    ref = (key[0], key[1], n)
                    if frontier.get(key, -1) >= n:
                        continue
                    frontier[key] = n
                    w.append(ref)
                for ref in w:
                    if ref[0] == "E":
                        needed[ref[1]].add(ref[2])
                op.waits = w
        pos = {}
        for eng in ENGINES:
            cnt = 0
            marks = needed[eng]
            for i, op in enumerate(self.ops[eng]):
                if i in marks:
                    if op.method is None:
                        raise RuntimeError(f"pure-wait op on {eng} is a dep target")
                    cnt += 1
                    if op.inc_pos is None:
                        op.inc_pos = ("E", eng)
                    pos[(eng, i)] = cnt
        for eng in ENGINES:
            for op in self.ops[eng]:
                tw = []
                for ref in op.waits:
                    if ref[0] == "E":
                        tw.append((self.eng_sems[ref[1]], pos[(ref[1], ref[2])]))
                    else:
                        tw.append((self.dma_slots[ref[1]], 16 * ref[2]))
                op.waits = tw

    def replay(self, block):
        def runner(eng_name):
            def _run(engine):
                for op in self.ops[eng_name]:
                    for (sem, val) in op.waits:
                        engine.wait_ge(sem, val)
                    if op.method is None:
                        continue
                    inst = getattr(engine, op.method)(*op.args, **op.kwargs)
                    if op.inc_pos is not None:
                        kind, tgt = op.inc_pos
                        if kind == "D":
                            inst.then_inc(self.dma_slots[tgt], 16)
                        else:
                            inst.then_inc(self.eng_sems[tgt], 1)
            return _run

        block.tensor(runner("tensor"))
        block.scalar(runner("scalar"))
        block.vector(runner("vector"))
        block.gpsimd(runner("gpsimd"))
        block.sync(runner("sync"))


def build_program(S=2048, BATCH=2, DM=1024, NH=16, dbg=False):
    D = DM // NH
    CK = DM // 128
    NP = NH // 2
    TB = S // 128
    SKB = S // 128
    SQC = min(1024, S)
    NSQ = S // SQC
    NQ = 4 if TB >= 4 else TB
    QTB = TB // NQ
    QT = QTB * 128
    MMN = min(512, S)
    NCH = S // MMN
    EXP_MARGIN = 2.0

    nc = bass.Bass("TRN2", target_bir_lowering=False)

    x_d = nc.dram_tensor("x", [BATCH, S, DM], F32, kind="ExternalInput").ap()
    wq_d = nc.dram_tensor("wq", [DM, DM], F32, kind="ExternalInput").ap()
    wk_d = nc.dram_tensor("wk", [DM, DM], F32, kind="ExternalInput").ap()
    wv_d = nc.dram_tensor("wv", [DM, DM], F32, kind="ExternalInput").ap()
    wo_d = nc.dram_tensor("wo", [DM, DM], F32, kind="ExternalInput").ap()
    pope_d = nc.dram_tensor("pope", [NH, D], F32, kind="ExternalInput").ap()
    w1_d = nc.dram_tensor("cnw1", [DM, DM], F32, kind="ExternalInput").ap()
    b1_d = nc.dram_tensor("cnb1", [DM], F32, kind="ExternalInput").ap()
    w2_d = nc.dram_tensor("cnw2", [DM, DM // 2], F32, kind="ExternalInput").ap()
    b2_d = nc.dram_tensor("cnb2", [DM // 2], F32, kind="ExternalInput").ap()
    w3_d = nc.dram_tensor("cnw3", [DM // 2, DM // 4], F32, kind="ExternalInput").ap()
    b3_d = nc.dram_tensor("cnb3", [DM // 4], F32, kind="ExternalInput").ap()
    w4_d = nc.dram_tensor("cnw4", [DM // 4, 1], F32, kind="ExternalInput").ap()
    b4_d = nc.dram_tensor("cnb4", [1], F32, kind="ExternalInput").ap()
    g1_d = nc.dram_tensor("gcw1", [DM, DM // 2], F32, kind="ExternalInput").ap()
    gb1_d = nc.dram_tensor("gcb1", [DM // 2], F32, kind="ExternalInput").ap()
    g2_d = nc.dram_tensor("gcw2", [DM // 2, DM // 4], F32, kind="ExternalInput").ap()
    gb2_d = nc.dram_tensor("gcb2", [DM // 4], F32, kind="ExternalInput").ap()
    g3_d = nc.dram_tensor("gcw3", [DM // 4, 1], F32, kind="ExternalInput").ap()
    gb3_d = nc.dram_tensor("gcb3", [1], F32, kind="ExternalInput").ap()
    ar_d = nc.dram_tensor("ar", [1], F32, kind="ExternalInput").ap()
    out_d = nc.dram_tensor("out", [BATCH, S, DM], F32, kind="ExternalOutput").ap()
    wT_s = {w: nc.dram_tensor(f"{w}T_s", [CK, 128, DM], BF16, kind="Internal").ap()
            for w in ("wq", "wk", "wv", "wo")}
    ffn_s = nc.dram_tensor("ffn_s", [BATCH, S, DM], F32, kind="Internal").ap()
    if dbg:
        dbg_xn = nc.dram_tensor("dbg_xn", [CK, 128, S], BF16, kind="ExternalOutput").ap()
        dbg_qh = nc.dram_tensor("dbg_qh", [2, 128, S], BF16, kind="ExternalOutput").ap()
        dbg_ke = nc.dram_tensor("dbg_ke", [2, 128, S], BF16, kind="ExternalOutput").ap()
        dbg_vp = nc.dram_tensor("dbg_vp", [128, (S // 128) * 130], BF16, kind="ExternalOutput").ap()
        dbg_rows = nc.dram_tensor("dbg_rows", [16, S], F32, kind="ExternalOutput").ap()
        dbg_expb = nc.dram_tensor("dbg_expb", [128, NH], F32, kind="ExternalOutput").ap()
        dbg_attn = nc.dram_tensor("dbg_attn", [CK, 128, S], BF16, kind="ExternalOutput").ap()

    inv = 10000.0 ** (-(np.arange(D, dtype=np.float64) / D))
    fr = np.arange(S, dtype=np.float64)[None, :] * inv[:, None]
    t1_np = np.concatenate([np.cos(fr), np.sin(fr)], 0).astype(ml_dtypes.bfloat16)
    t1s_np = np.concatenate([np.sin(fr), np.cos(fr)], 0).astype(ml_dtypes.bfloat16)
    T1_d = nc.inline_tensor(t1_np, "T1c").ap()
    T1s_d = nc.inline_tensor(t1s_np, "T1sc").ap()
    ident_d = nc.inline_tensor(np.eye(128, dtype=ml_dtypes.bfloat16), "identc").ap()
    i16_d = nc.inline_tensor(np.eye(16, dtype=np.float32), "i16c").ap()
    onesf_d = nc.inline_tensor(np.ones((1, 128), dtype=np.float32), "onesfc").ap()
    o2 = np.zeros((128, 2), dtype=ml_dtypes.bfloat16)
    o2[0:64, 0] = 1.0
    o2[64:128, 1] = 1.0
    ones2_d = nc.inline_tensor(o2, "ones2c").ap()

    A = nc.alloc_sbuf_tensor
    cpool = A("cpool", [128, 5], F32)
    T1 = A("T1", [128, S], BF16)
    T1s = A("T1s", [128, S], BF16)
    ident = A("ident", [128, 128], BF16)
    i16 = A("i16", [16, 16], F32)
    onesf = A("onesf", [1, 128], F32)
    ones2 = A("ones2", [128, 2], BF16)
    pbt = A("pbt", [16, D], F32)
    bcl = A("bcl", [16, D], F32)
    btr = A("btr", [D, 16], F32)
    sbv = A("sbv", [D, 16], F32)
    cbv = A("cbv", [D, 16], F32)
    ttv = A("ttv", [D, 16], F32)
    cbdup = A("cbdup", [128, NH], F32)
    sbdup = A("sbdup", [128, NH], F32)
    mstat = A("mstat", [128, TB], F32)
    rstat = A("rstat", [128, TB], F32)
    mr = A("mr", [128, TB], F32)
    rowsums = A("rowsums", [16, S], F32)
    rcps = A("rcps", [1, SQC], F32)
    gcv = A("gcv", [128, CK], F32)
    gcvb = A("gcvb", [128, CK], BF16)
    b1f = A("b1f", [128, CK], F32)
    b2f = A("b2f", [128, CK // 2], F32)
    b3f = A("b3f", [128, CK // 4], F32)
    gbf1 = A("gbf1", [128, CK // 2], F32)
    gbf2 = A("gbf2", [128, CK // 4], F32)
    smallt = A("smallt", [1, 8], F32)
    tmpB = [A(f"tmpB{i}", [64, SQC], BF16) for i in range(2)]
    rowstg0 = A("rowstg0", [65, SQC], F32)
    woTt = [A(f"woTtp{c}", [128, DM], BF16) for c in range(CK)]
    w3t = [A(f"w3t{k}", [128, DM // 4], BF16) for k in range(4)]
    w4t = [A(f"w4t{k}", [128, 1], BF16) for k in range(2)]
    gw2t = [A(f"gw2t{k}", [128, DM // 4], BF16) for k in range(4)]
    gw3t = [A(f"gw3t{k}", [128, 1], BF16) for k in range(2)]
    hg1 = A("hg1", [128, 4], BF16)
    hg2 = A("hg2", [128, 2], BF16)
    xn_fm = [A(f"xnfm{c}", [128, S], BF16) for c in range(CK)]
    attn_in = [A(f"attnin{c}", [128, S], BF16) for c in range(CK)]
    xt = [A(f"xt{i}", [128, DM], F32) for i in range(2)]
    xsq = A("xsq", [128, DM], F32)
    v1t = [A(f"v1t{i}", [128, 1], F32) for i in range(2)]

    def rnd(x):
        return (x + 63) // 64 * 64

    lay = {}

    def place(name, shape, dtype, cursor):
        nbytes = shape[1] * (2 if dtype == BF16 else 4)
        lay[name] = (shape, dtype, cursor)
        return cursor + rnd(nbytes)

    arena_sz = 0
    cur = 0
    cur = place("qmag", [128, S], BF16, cur)
    cur = place("kmag", [128, S], BF16, cur)
    cur = place("tmp1", [128, S], BF16, cur)
    cur = place("tmpe", [128, max(MMN, DM)], F32, cur)
    for b in range(2):
        for h in range(2):
            cur = place(f"qh{b}_{h}", [128, S], BF16, cur)
            cur = place(f"ke{b}_{h}", [128, S], BF16, cur)
    for b in range(2):
        cur = place(f"vp{b}", [128, SKB * 130], BF16, cur)
    for b in range(3):
        cur = place(f"pt{b}", [128, SQC], BF16, cur)
    for b in range(2):
        for w in ("wq", "wk", "wv"):
            cur = place(f"{w}t{b}", [128, DM], BF16, cur)
    cur = place("tmpv0", [128, MMN], BF16, cur)
    cur = place("tmpv1", [128, MMN], BF16, cur)
    cur = place("xnb0", [128, DM], BF16, cur)
    cur = place("xnb1", [128, DM], BF16, cur)
    arena_sz = max(arena_sz, cur)
    VIEW_PROJ = ["qmag", "kmag", "tmp1", "tmpe", "tmpv0", "tmpv1",
                 "xnb0", "xnb1", "vp0", "vp1", "pt0", "pt1", "pt2"] \
        + [f"qh{b}_{h}" for b in range(2) for h in range(2)] \
        + [f"ke{b}_{h}" for b in range(2) for h in range(2)] \
        + [f"{w}t{b}" for b in range(2) for w in ("wq", "wk", "wv")]

    cur = 0
    cur = place("stream", [128, QTB * DM], F32, cur)
    for c in range(CK):
        cur = place(f"ofm{c}", [128, QT], BF16, cur)
    cur = place("sbf0", [128, DM], BF16, cur)
    cur = place("sbf1", [128, DM], BF16, cur)
    for c in range(CK):
        cur = place(f"h1_{c}", [128, QT], BF16, cur)
    for c in range(CK // 2):
        cur = place(f"h2_{c}", [128, QT], BF16, cur)
    for c in range(CK // 4):
        cur = place(f"h3_{c}", [128, QT], BF16, cur)
    cur = place("t01", [128, 512], F32, cur)
    cur = place("cpt", [1, QT], F32, cur)
    cur = place("cptm", [128, QTB], F32, cur)
    cur = place("ncptm", [128, QTB], F32, cur)
    for k in range(CK):
        cur = place(f"w1t_{k}", [128, DM], BF16, cur)
        cur = place(f"w2t_{k}", [128, DM // 2], BF16, cur)
        cur = place(f"gw1t_{k}", [128, DM // 2], BF16, cur)
    arena_sz = max(arena_sz, cur)
    VIEW_FFN = ([f"stream{i}" for i in range(QTB)]
                + ["sbf0", "sbf1", "t01", "cpt", "cptm", "ncptm"]
                + [f"ofm{c}" for c in range(CK)]
                + [f"h1_{c}" for c in range(CK)]
                + [f"h2_{c}" for c in range(CK // 2)]
                + [f"h3_{c}" for c in range(CK // 4)]
                + [f"w1t_{k}" for k in range(CK)]
                + [f"w2t_{k}" for k in range(CK)]
                + [f"gw1t_{k}" for k in range(CK)])

    VIEW_WO = []

    base, _ = nc.bump_sbuf(arena_sz)
    T = {}
    for name, (shape, dtype, off) in lay.items():
        T[name] = nc.alloc_sbuf_tensor_at(name, shape, dtype, offset=base + off)
    rcpsB = nc.alloc_sbuf_tensor_at("rcpsB", [1, SQC], F32, offset=base + lay["pt0"][2])

    ps_s = [nc.alloc_psum_tensor("ps_s0", [128, 1024], F32),
            nc.alloc_psum_tensor("ps_s1", [128, 1024], F32)]
    ps_pv = nc.alloc_psum_tensor("ps_pv", [65, 1024], F32)
    ps_w = [nc.alloc_psum_tensor("ps_w0", [128, 512], F32),
            nc.alloc_psum_tensor("ps_w1", [128, 512], F32)]
    KW = ["ps_w0", "ps_w1"]
    KS = ["ps_s0", "ps_s1"]

    P = Prog(nc)
    E = P.emit

    def dma(queue, dst, src, reads, writes, note="", no_waw=False, **kw):
        return E(queue, "dma_start", dst, src, reads=reads, writes=writes, dma=True,
                 no_waw=no_waw, note=note, **kw)

    E("vector", "memset", cpool[:, 0:1], 1e-5, writes=["cpool"])
    E("vector", "memset", cpool[:, 1:2], 0.5 * PI, reads=["cpool"], writes=["cpool"])
    E("vector", "memset", cpool[:, 2:3], PI, reads=["cpool"], writes=["cpool"])
    E("vector", "memset", cpool[:, 3:4], 1.0, reads=["cpool"], writes=["cpool"])
    E("vector", "memset", cpool[:, 4:5], -8.0, reads=["cpool"], writes=["cpool"])
    dma("sync", T1[:], T1_d, reads=["dT1"], writes=["consts"], no_waw=True)
    dma("sync", T1s[:], T1s_d, reads=["dT1s"], writes=["consts"], no_waw=True)
    dma("sync", ident[:], ident_d, reads=["did"], writes=["consts"], no_waw=True)
    dma("sync", i16[:], i16_d, reads=["di16"], writes=["consts"], no_waw=True)
    dma("sync", onesf[:], onesf_d, reads=["donesf"], writes=["consts"], no_waw=True)
    dma("sync", ones2[:], ones2_d, reads=["dones2"], writes=["consts"], no_waw=True)
    dma("sync", pbt[:], pope_d, reads=["dpope"], writes=["pbt"])
    dma("sync", b1f[:], b1_d.rearrange("(j p) -> p j", p=128), reads=["db1"], writes=["consts"], no_waw=True, allow_slow_non_contiguous=True)
    dma("sync", b2f[:], b2_d.rearrange("(j p) -> p j", p=128), reads=["db2"], writes=["consts"], no_waw=True, allow_slow_non_contiguous=True)
    dma("sync", b3f[:], b3_d.rearrange("(j p) -> p j", p=128), reads=["db3"], writes=["consts"], no_waw=True, allow_slow_non_contiguous=True)
    dma("sync", gbf1[:], gb1_d.rearrange("(j p) -> p j", p=128), reads=["dgb1"], writes=["consts"], no_waw=True, allow_slow_non_contiguous=True)
    dma("sync", gbf2[:], gb2_d.rearrange("(j p) -> p j", p=128), reads=["dgb2"], writes=["consts"], no_waw=True, allow_slow_non_contiguous=True)
    dma("sync", smallt[0:1, 0:1], b4_d[None, :], reads=["db4"], writes=["smallt"])
    dma("sync", smallt[0:1, 1:2], gb3_d[None, :], reads=["dgb3"], writes=["smallt"])
    dma("sync", smallt[0:1, 2:3], ar_d[None, :], reads=["dar"], writes=["smallt"])
    E("vector", "tensor_scalar", smallt[0:1, 0:2], smallt[0:1, 0:2], 0.5, None, ALU.mult,
      reads=["smallt"], writes=["smallt"])
    def cast_load(dst_ap, dst_key, src_ap, src_key, width, stg_i):
        dma("gpsimd", dst_ap, src_ap, reads=[src_key], writes=[dst_key])

    for k in range(4):
        cast_load(w3t[k][:], f"w3t{k}", w3_d[k * 128:(k + 1) * 128, :], "dw3", DM // 4, k % 2)
        cast_load(gw2t[k][:], f"gw2t{k}", g2_d[k * 128:(k + 1) * 128, :], "dg2", DM // 4, k % 2)
    for k in range(2):
        cast_load(w4t[k][:], f"w4t{k}", w4_d[k * 128:(k + 1) * 128, :], "dw4", 1, k % 2)
        cast_load(gw3t[k][:], f"gw3t{k}", g3_d[k * 128:(k + 1) * 128, :], "dg3", 1, k % 2)

    E("vector", "tensor_scalar", bcl[:], pbt[:], 0.0, -2 * PI, ALU.min, ALU.max,
      reads=["pbt"], writes=["bcl"])
    E("tensor", "matmul", ps_w[0][0:D, 0:16], bcl[:], i16[:], start=True, stop=True,
      reads=["bcl", "consts"], writes=[KW[0]])
    E("vector", "tensor_copy", btr[:], ps_w[0][0:D, 0:16], reads=[KW[0]], writes=["btr"])
    E("scalar", "activation", sbv[:], btr[:], AF.Sin, bias=cpool[0:D, 2:3], scale=1.0,
      reads=["btr", "cpool"], writes=["sbv"])
    E("scalar", "activation", ttv[:], btr[:], AF.Abs, bias=cpool[0:D, 2:3], scale=1.0,
      reads=["btr", "cpool"], writes=["ttv"])
    E("scalar", "activation", cbv[:], ttv[:], AF.Sin, bias=cpool[0:D, 1:2], scale=-1.0,
      reads=["ttv", "cpool"], writes=["cbv"])
    E("vector", "tensor_scalar", cbv[:], cbv[:], -1.0, None, ALU.mult, reads=["cbv"], writes=["cbv"])
    E("vector", "tensor_scalar", sbv[:], sbv[:], -1.0, None, ALU.mult, reads=["sbv"], writes=["sbv"])
    dma("sync", cbdup[0:64, :], cbv[:], reads=["cbv"], writes=["cbdup"])
    dma("sync", cbdup[64:128, :], cbv[:], reads=["cbv"], writes=["cbdup"])
    dma("sync", sbdup[0:64, :], sbv[:], reads=["sbv"], writes=["sbdup"])
    E("vector", "tensor_scalar", ttv[:], sbv[:], -1.0, None, ALU.mult, reads=["sbv"], writes=["ttv"])
    dma("sync", sbdup[64:128, :], ttv[:], reads=["ttv"], writes=["sbdup"])

    if S * 2 >= 4 * DM:
        prepCb = nc.alloc_sbuf_tensor_at("prepCb", [128, DM], BF16, offset=base + lay["qh1_0"][2])
        prepDb = nc.alloc_sbuf_tensor_at("prepDb", [128, DM], BF16, offset=base + lay["qh1_1"][2])
    else:
        prepCb = prepDb = None

    def gen_prep(olap):
        names = {"wq": wq_d, "wk": wk_d, "wv": wv_d, "wo": wo_d}
        if olap:
            order = [(w, i) for i in range(CK) for w in ("wq", "wk", "wv")] \
                + [("wo", i) for i in range(CK)]
        else:
            order = [(w, i) for w in ("wq", "wk", "wv", "wo") for i in range(CK)]
        if olap:
            bufs = [(T["wqt1"], "wqt1"), (T["wkt1"], "wkt1"),
                    (prepCb, "qh1_0"), (prepDb, "qh1_1")]
            nb = 4
            stg4 = [(T["ke1_0"][:, 0:DM], ["stgA", "ke1_0"]),
                    (T["ke1_1"][:, 0:DM], ["stgB", "ke1_1"]),
                    (T["ke1_0"][:, DM:2 * DM], ["stgC", "ke1_0"]),
                    (T["ke1_1"][:, DM:2 * DM], ["stgD", "ke1_1"])]
            ns = 4
        elif S >= DM:
            bufs = [(T["qh0_0"], "qh0_0"), (T["qh0_1"], "qh0_1")]
            nb = 2
        else:
            bufs = [(T["xnb0"], "xnb0"), (T["xnb1"], "xnb1")]
            nb = 2
        for t, (wname, i) in enumerate(order):
            wd = names[wname]
            if True:
                wtmp_ap, wtmp = bufs[t % nb]
                if olap:
                    stg_ap, stg_keys = stg4[t % ns]
                else:
                    stg = "wqt0" if t % 2 == 0 else "wqt1"
                    stg_ap, stg_keys = T[stg][:, 0:DM], [stg]
                dma("gpsimd", wtmp_ap[:, 0:DM], wd[i * 128:(i + 1) * 128, :],
                    reads=[f"d{wname}"], writes=[f"prepc{t % nb}", wtmp])
                yield
                for j2 in range(CK // 4):
                    pw = j2 % 2
                    for jj in range(4):
                        j = j2 * 4 + jj
                        E("tensor", "matmul", ps_w[pw][:, jj * 128:(jj + 1) * 128],
                          wtmp_ap[:, j * 128:(j + 1) * 128], ident[:], start=True, stop=True,
                          reads=[wtmp, "consts"], writes=[KW[pw]])
                    E("vector", "tensor_copy", stg_ap[:, j2 * 512:(j2 + 1) * 512], ps_w[pw][:, :],
                      reads=[KW[pw]], writes=stg_keys, no_waw=len(stg_keys) > 1)
                    yield
                if wname == "wo":
                    for j in range(CK):
                        dma("sync", wT_s[wname][j][:, i * 128:(i + 1) * 128],
                            stg_ap[:, j * 128:(j + 1) * 128],
                            reads=stg_keys, writes=[f"{wname}Ts{i % 2}"], no_waw=True)
                else:
                    dma("sync", wT_s[wname][i], stg_ap[:, 0:DM], reads=stg_keys,
                        writes=[f"{wname}Ts{i % 2}"], no_waw=True)
                yield

    def xnk(tok_lo, n_tok):
        g_lo, g_hi = tok_lo // MMN, (tok_lo + n_tok - 1) // MMN
        return [f"xnfm{c}g{g}" for c in range(CK) for g in range(g_lo, g_hi + 1)]

    def drive(*pairs):
        gens = [g for g, n in pairs]
        rates = [n for g, n in pairs]
        done = [False] * len(gens)
        while not all(done):
            for gi in range(len(gens)):
                for _ in range(rates[gi]):
                    if not done[gi]:
                        done[gi] = next(gens[gi], _SENT) is _SENT
    _SENT = object()

    for s in range(BATCH):
        P.alias_barrier(VIEW_FFN, VIEW_PROJ)

        def gen_ln():
            for tb in range(TB):
                xb = f"xt{tb % 2}"
                xb_t = xt[tb % 2]
                dma("sync", xb_t[:], x_d[s, tb * 128:(tb + 1) * 128, :], reads=["dx"], writes=[xb])
                E("vector", "reduce_sum", mstat[:, tb:tb + 1], xb_t[:], axis=AX.X,
                  reads=[xb], writes=["mstat"])
                E("scalar", "activation", xsq[:], xb_t[:], AF.Square, bias=0.0, scale=1.0,
                  reads=[xb], writes=["xsq"])
                E("vector", "reduce_sum", rstat[:, tb:tb + 1], xsq[:], axis=AX.X,
                  reads=["xsq"], writes=["rstat"])
                yield
            E("vector", "tensor_scalar", mstat[:, :], mstat[:, :], 1.0 / DM, None, ALU.mult,
              reads=["mstat"], writes=["mstat"])
            E("vector", "tensor_mul", mr[:, :], mstat[:, :], mstat[:, :],
              reads=["mstat"], writes=["mr"])
            E("vector", "tensor_scalar", rstat[:, :], rstat[:, :], 1.0 / DM, None, ALU.mult,
              reads=["rstat"], writes=["rstat"])
            E("vector", "tensor_sub", rstat[:, :], rstat[:, :], mr[:, :],
              reads=["rstat", "mr"], writes=["rstat"])
            E("scalar", "activation", rstat[:, :], rstat[:, :], AF.Ln, bias=cpool[:, 0:1],
              scale=1.0, reads=["rstat", "cpool"], writes=["rstat"])
            E("scalar", "activation", rstat[:, :], rstat[:, :], AF.Exp, bias=0.0, scale=-0.5,
              reads=["rstat"], writes=["rstat"])
            E("vector", "scalar_tensor_tensor", mr[:, :], mstat[:, :], -1.0, rstat[:, :],
              ALU.mult, ALU.mult, reads=["mstat", "rstat"], writes=["mr"])
            yield
            for tb in range(TB):
                xb = f"xt{tb % 2}"
                xb_t = xt[tb % 2]
                dma("sync", xb_t[:], x_d[s, tb * 128:(tb + 1) * 128, :], reads=["dx"], writes=[xb])
                xnb = "xnb0" if tb % 2 == 0 else "xnb1"
                E("vector", "tensor_scalar", T[xnb][:], xb_t[:], mstat[:, tb:tb + 1],
                  rstat[:, tb:tb + 1], ALU.subtract, ALU.mult,
                  reads=[xb, "mstat", "rstat"], writes=[xnb])
                yield
                for c2 in range(CK // 4):
                    pw = (tb * 2 + c2) % 2
                    for cc in range(4):
                        c = c2 * 4 + cc
                        E("tensor", "matmul", ps_w[pw][:, cc * 128:(cc + 1) * 128],
                          T[xnb][:, c * 128:(c + 1) * 128], ident[:], start=True, stop=True,
                          reads=[xnb, "consts"], writes=[KW[pw]])
                    eng2 = "scalar" if pw == 0 else "vector"
                    meth = "copy" if eng2 == "scalar" else "tensor_copy"
                    for cc in range(4):
                        c = c2 * 4 + cc
                        E(eng2, meth, xn_fm[c][:, tb * 128:(tb + 1) * 128],
                          ps_w[pw][:, cc * 128:(cc + 1) * 128],
                          reads=[KW[pw]], writes=[f"xnfm{c}g{tb * 128 // MMN}"])
                yield

        def gen_projembed(j, midyield=True):
            B = j % 2
            hA = 2 * j
            for wname in ("wq", "wk", "wv"):
                dma("sync", T[f"{wname}t{B}"][:], wT_s[wname][j],
                    reads=[f"{wname}Ts0", f"{wname}Ts1"], writes=[f"{wname}t{B}"])
            yield
            for wname, magk in (("wq", "qmag"), ("wk", "kmag")):
                for ch in range(NCH):
                    pw = ch % 2
                    for c in range(CK):
                        E("tensor", "matmul", ps_w[pw][:, 0:MMN],
                          T[f"{wname}t{B}"][:, c * 128:(c + 1) * 128],
                          xn_fm[c][:, ch * MMN:(ch + 1) * MMN],
                          start=(c == 0), stop=(c == CK - 1),
                          reads=[f"{wname}t{B}"] + xnk(ch * MMN, MMN), writes=[KW[pw]])
                        if c == 3 and midyield:
                            yield
                    E("scalar", "copy", T[magk][:, ch * MMN:(ch + 1) * MMN],
                      ps_w[pw][:, 0:MMN], reads=[KW[pw]], writes=[magk])
                    yield
                SPW = min(S, 1024)
                for o_ in range(0, S, SPW):
                    E("scalar", "activation", T["tmpe"][:, 0:SPW], T[magk][:, o_:o_ + SPW],
                      AF.Exp, bias=0.0, scale=1.0, reads=[magk], writes=["tmpe"])
                    E("scalar", "activation", T[magk][:, o_:o_ + SPW], T["tmpe"][:, 0:SPW],
                      AF.Ln, bias=cpool[:, 3:4], scale=1.0, reads=["tmpe", "cpool"], writes=[magk])
                    yield
            for ch in range(NCH):
                pw = ch % 2
                for c in range(CK):
                    E("tensor", "matmul", ps_w[pw][:, 0:MMN],
                      T[f"wvt{B}"][:, c * 128:(c + 1) * 128],
                      xn_fm[c][:, ch * MMN:(ch + 1) * MMN],
                      start=(c == 0), stop=(c == CK - 1),
                      reads=[f"wvt{B}"] + xnk(ch * MMN, MMN), writes=[KW[pw]])
                    if c == 3 and midyield:
                        yield
                tv = f"tmpv{pw}"
                E("vector", "tensor_copy", T[tv][:, 0:MMN], ps_w[pw][:, 0:MMN],
                  reads=[KW[pw]], writes=[tv])
                yield
                if True:
                    pass
                for bb in range(MMN // 128):
                    blk = ch * (MMN // 128) + bb
                    pv2 = (ch + 1) % 2
                    E("tensor", "matmul", ps_w[pv2][:, (bb % 4) * 128:(bb % 4 + 1) * 128],
                      T[tv][:, bb * 128:(bb + 1) * 128], ident[:], start=True, stop=True,
                      reads=[tv, "consts"], writes=[KW[pv2]])
                    E("vector", "tensor_copy", T[f"vp{B}"][:, blk * 130:blk * 130 + 64],
                      ps_w[pv2][:, (bb % 4) * 128:(bb % 4) * 128 + 64],
                      reads=[KW[pv2]], writes=[f"vp{B}"])
                    E("vector", "tensor_copy", T[f"vp{B}"][:, blk * 130 + 65:blk * 130 + 129],
                      ps_w[pv2][:, (bb % 4) * 128 + 64:(bb % 4) * 128 + 128],
                      reads=[KW[pv2]], writes=[f"vp{B}"])
                yield
            vpr = T[f"vp{B}"].ap().rearrange("p (b c) -> p b c", c=130)
            E("vector", "memset", vpr[:, :, 64:65], 1.0, reads=[f"vp{B}"], writes=[f"vp{B}"])
            E("vector", "memset", vpr[:, :, 129:130], 1.0, reads=[f"vp{B}"], writes=[f"vp{B}"])
            yield
            for hh in range(2):
                h = 2 * j + hh
                kek = f"ke{B}_{hh}"
                ket = T[kek]
                qhk = f"qh{B}_{hh}"
                qht = T[qhk]
                if hh == 0:
                    E("vector", "tensor_mul", ket[0:64, :], T["kmag"][0:64, :], T1[0:64, :],
                      reads=["kmag", "consts"], writes=[kek])
                    dma("sync", ket[64:128, :], T["kmag"][0:64, :], reads=["kmag"], writes=[kek])
                    E("vector", "tensor_mul", ket[64:128, :], ket[64:128, :], T1[64:128, :],
                      reads=[kek, "consts"], writes=[kek])
                    yield
                    E("vector", "tensor_mul", T["tmp1"][0:64, :], T["qmag"][0:64, :], T1[0:64, :],
                      reads=["qmag", "consts"], writes=["tmp1"])
                    E("vector", "tensor_mul", qht[0:64, :], T["qmag"][0:64, :], T1s[0:64, :],
                      reads=["qmag", "consts"], writes=[qhk])
                    dma("sync", qht[64:128, :], T["qmag"][0:64, :], reads=["qmag"], writes=[qhk])
                    dma("sync", T["tmp1"][64:128, :], T["qmag"][0:64, :], reads=["qmag"], writes=["tmp1"])
                    yield
                    E("vector", "tensor_mul", T["tmp1"][64:128, :], T["tmp1"][64:128, :], T1[64:128, :],
                      reads=["tmp1", "consts"], writes=["tmp1"])
                    E("vector", "tensor_mul", qht[64:128, :], qht[64:128, :], T1s[64:128, :],
                      reads=[qhk, "consts"], writes=[qhk])
                    yield
                else:
                    dma("sync", ket[0:64, :], T["kmag"][64:128, :], reads=["kmag"], writes=[kek])
                    E("vector", "tensor_mul", ket[0:64, :], ket[0:64, :], T1[0:64, :],
                      reads=[kek, "consts"], writes=[kek])
                    E("vector", "tensor_mul", ket[64:128, :], T["kmag"][64:128, :], T1[64:128, :],
                      reads=["kmag", "consts"], writes=[kek])
                    yield
                    dma("sync", qht[0:64, :], T["qmag"][64:128, :], reads=["qmag"], writes=[qhk])
                    dma("sync", T["tmp1"][0:64, :], T["qmag"][64:128, :], reads=["qmag"], writes=["tmp1"])
                    E("vector", "tensor_mul", T["tmp1"][0:64, :], T["tmp1"][0:64, :], T1[0:64, :],
                      reads=["tmp1", "consts"], writes=["tmp1"])
                    E("vector", "tensor_mul", qht[0:64, :], qht[0:64, :], T1s[0:64, :],
                      reads=[qhk, "consts"], writes=[qhk])
                    yield
                    E("vector", "tensor_mul", T["tmp1"][64:128, :], T["qmag"][64:128, :], T1[64:128, :],
                      reads=["qmag", "consts"], writes=["tmp1"])
                    E("vector", "tensor_mul", qht[64:128, :], T["qmag"][64:128, :], T1s[64:128, :],
                      reads=["qmag", "consts"], writes=[qhk])
                    yield
                E("vector", "tensor_scalar", T["tmp1"][:, :], T["tmp1"][:, :], cbdup[:, h:h + 1],
                  None, ALU.mult, reads=["tmp1", "cbdup"], writes=["tmp1"])
                E("vector", "scalar_tensor_tensor", qht[:, :], qht[:, :], sbdup[:, h:h + 1],
                  T["tmp1"][:, :], ALU.mult, ALU.add, reads=[qhk, "tmp1", "sbdup"], writes=[qhk])
                yield
            yield

        if s == 0:
            if S * 2 >= 4 * DM:
                gl, gp = gen_ln(), gen_prep(True)
                ln_done = False
                while not ln_done:
                    for _ in range(2):
                        if not ln_done:
                            ln_done = next(gl, _SENT) is _SENT
                    next(gp, None)
                drive((gp, 3), (gen_projembed(0, midyield=False), 1))
            else:
                for _ in gen_prep(False):
                    pass
                for _ in gen_ln():
                    pass
        else:
            for _ in gen_ln():
                pass

        for c in range(CK):
            E("vector", "reduce_sum", gcv[:, c:c + 1], xn_fm[c][:, :], axis=AX.X,
              reads=xnk(0, S), writes=["gcv"])

        def emit_scores(jp, hh, sq, blk, pp):
            B = jp % 2
            qhk, kek = f"qh{B}_{hh}", f"ke{B}_{hh}"
            for m in range(SQC // MMN):
                E("tensor", "matmul", ps_s[pp][:, m * MMN:(m + 1) * MMN],
                  T[kek][:, blk * 128:(blk + 1) * 128],
                  T[qhk][:, sq * SQC + m * MMN: sq * SQC + (m + 1) * MMN],
                  start=True, stop=True, reads=[kek, qhk], writes=[KS[pp]])

        def emit_exp(pp, ptb):
            E("scalar", "activation", T[f"pt{ptb}"][:, 0:SQC], ps_s[pp][:, 0:SQC],
              AF.Exp, bias=cpool[:, 4:5], scale=0.125,
              reads=[KS[pp], "cpool"], writes=[f"pt{ptb}"])

        def emit_pv(jp, hh, sq, blk, ptb):
            B = jp % 2
            for m in range(SQC // MMN):
                E("tensor", "matmul", ps_pv[:, m * MMN:(m + 1) * MMN],
                  T[f"vp{B}"][:, blk * 130 + hh * 65: blk * 130 + hh * 65 + 65],
                  T[f"pt{ptb}"][:, m * MMN:(m + 1) * MMN],
                  start=(blk == 0), stop=(blk == SKB - 1),
                  reads=[f"vp{B}", f"pt{ptb}"], writes=["ps_pv"])

        def emit_evac(jp, hh, sq):
            h = 2 * jp + hh
            rsi = (h * NSQ + sq) % 2
            if rsi == 0:
                rs_ap, rsk = rowstg0[64:65, 0:SQC], "rowstg0"
            else:
                rs_ap, rsk = xsq[64:65, 0:SQC], "xsq"
            E("vector", "tensor_copy", rs_ap, ps_pv[64:65, 0:SQC],
              reads=["ps_pv"], writes=[rsk])
            dma("sync", rowsums[h:h + 1, sq * SQC:(sq + 1) * SQC],
                rs_ap, reads=[rsk], writes=["rowsums"])
            if hh == 0:
                E("vector", "tensor_copy", attn_in[jp][0:64, sq * SQC:(sq + 1) * SQC],
                  ps_pv[0:64, 0:SQC], reads=["ps_pv"], writes=[f"attnin{jp}"])
            else:
                tBk = f"tmpB{sq % 2}"
                tBt = tmpB[sq % 2]
                E("vector", "tensor_copy", tBt[:, 0:SQC], ps_pv[0:64, 0:SQC],
                  reads=["ps_pv"], writes=[tBk])
                dma("sync", attn_in[jp][64:128, sq * SQC:(sq + 1) * SQC], tBt[:, 0:SQC],
                    reads=[tBk], writes=[f"attnin{jp}"])

        att_pend = []

        def gen_attention(j):
            steps = [(hh, sq, blk) for hh in range(2) for sq in range(NSQ)
                     for blk in range(SKB)]
            nsp = len(steps)
            for i, (hh, sq, blk) in enumerate(steps):
                gi = j * nsp + i
                pp = gi % 2
                ptb = gi % 3
                emit_scores(j, hh, sq, blk, pp)
                emit_exp(pp, ptb)
                if len(att_pend) == 2:
                    oj, ohh, osq, oblk, optb = att_pend.pop(0)
                    emit_pv(oj, ohh, osq, oblk, optb)
                    if oblk == SKB - 1:
                        emit_evac(oj, ohh, osq)
                att_pend.append((j, hh, sq, blk, ptb))
                yield
            if j == NP - 1:
                while att_pend:
                    oj, ohh, osq, oblk, optb = att_pend.pop(0)
                    emit_pv(oj, ohh, osq, oblk, optb)
                    if oblk == SKB - 1:
                        emit_evac(oj, ohh, osq)

        if not (s == 0 and S * 2 >= 4 * DM):
            for _ in gen_projembed(0):
                pass
        for j in range(NP):
            ga = gen_attention(j)
            gp = gen_projembed(j + 1) if j + 1 < NP else iter(())
            drive((ga, 1), (gp, 1))
            if dbg and j == 0 and s == 0:
                for hh in range(2):
                    dma("sync", dbg_qh[hh], T[f"qh0_{hh}"][:], reads=[f"qh0_{hh}"], writes=["dbgqh"])
                    dma("sync", dbg_ke[hh], T[f"ke0_{hh}"][:], reads=[f"ke0_{hh}"], writes=["dbgke"])
                dma("sync", dbg_vp, T["vp0"][:], reads=["vp0"], writes=["dbgvp"])

        def gen_norm():
            E("vector", "reciprocal", rowsums[:, :], rowsums[:, :],
              reads=["rowsums"], writes=["rowsums"])
            for h in range(NH):
                j, hh = h // 2, h % 2
                for sq in range(NSQ):
                    pp = (h * NSQ + sq) % 2
                    lo = hh * 64
                    if pp == 0:
                        rc_ap, rck = rcps[0:1, 0:SQC], "rcps"
                    else:
                        rc_ap, rck = rcpsB[0:1, 0:SQC], "rcpsB"
                    dma("sync", rc_ap, rowsums[h:h + 1, sq * SQC:(sq + 1) * SQC],
                        reads=["rowsums"], writes=[rck, "pt0", "pt1"] if pp else [rck])
                    rrd = [rck, "pt0", "pt1", "consts"] if pp else [rck, "consts"]
                    for m in range(SQC // MMN):
                        E("tensor", "matmul", ps_s[pp][lo:lo + 64, m * MMN:(m + 1) * MMN],
                          onesf[0:1, 0:64], rc_ap[0:1, m * MMN:(m + 1) * MMN],
                          start=True, stop=True, reads=rrd, writes=[KS[pp]])
                    E("vector", "tensor_mul", attn_in[j][lo:lo + 64, sq * SQC:(sq + 1) * SQC],
                      attn_in[j][lo:lo + 64, sq * SQC:(sq + 1) * SQC], ps_s[pp][lo:lo + 64, 0:SQC],
                      reads=[f"attnin{j}", KS[pp]], writes=[f"attnin{j}"])
                    yield

        if dbg and s == 0:
            for c in range(CK):
                dma("sync", dbg_xn[c], xn_fm[c][:], reads=xnk(0, S), writes=["dbgxn"])
                dma("sync", dbg_attn[c], attn_in[c][:], reads=[f"attnin{c}"], writes=["dbgattn"])
            dma("sync", dbg_rows, rowsums[:], reads=["rowsums"], writes=["dbgrows"])

        def gen_gc():
            for k in range(CK):
                xg = xt[k % 2]
                dma("sync", xg[:, 0:DM // 2], g1_d[k * 128:(k + 1) * 128, :],
                    reads=["dg1"], writes=[f"xt{k % 2}"])
                E("tensor", "matmul", ps_w[0][0:1, 0:DM // 2], gcv[:, k:k + 1],
                  xg[:, 0:DM // 2], start=(k == 0), stop=(k == CK - 1),
                  reads=[f"xt{k % 2}", "gcv"], writes=[KW[0]])
                yield
            E("vector", "tensor_copy", xt[1][0:1, 0:DM // 2], ps_w[0][0:1, 0:DM // 2],
              reads=[KW[0]], writes=["xt1"])
            for ob in range(4):
                E("tensor", "matmul", ps_w[1][:, ob:ob + 1],
                  xt[1][0:1, ob * 128:(ob + 1) * 128], onesf[0:1, 0:1],
                  start=True, stop=True, reads=["xt1", "consts"], writes=[KW[1]])
            for ob in range(4):
                E("scalar", "activation", hg1[:, ob:ob + 1], ps_w[1][:, ob:ob + 1], AF.Gelu,
                  bias=gbf1[:, ob:ob + 1], scale=1.0 / S, reads=[KW[1], "consts"], writes=["hg1"])
            yield
            for ob in range(2):
                for k in range(4):
                    E("tensor", "matmul", ps_w[1][:, ob:ob + 1], gw2t[k][:, ob * 128:(ob + 1) * 128],
                      hg1[:, k:k + 1], start=(k == 0), stop=(k == 3),
                      reads=[f"gw2t{k}", "hg1"], writes=[KW[1]])
            for ob in range(2):
                E("scalar", "activation", hg2[:, ob:ob + 1], ps_w[1][:, ob:ob + 1], AF.Gelu,
                  bias=gbf2[:, ob:ob + 1], scale=1.0, reads=[KW[1], "consts"], writes=["hg2"])
            yield
            for k in range(2):
                E("tensor", "matmul", ps_w[0][0:1, 4:5], gw3t[k][:, 0:1], hg2[:, k:k + 1],
                  start=(k == 0), stop=(k == 1), reads=[f"gw3t{k}", "hg2"], writes=[KW[0]])
            E("scalar", "activation", smallt[0:1, 3:4], ps_w[0][0:1, 4:5], AF.Tanh,
              bias=smallt[0:1, 1:2], scale=0.5, reads=[KW[0], "smallt"], writes=["smallt2"])
            yield
            E("vector", "tensor_mul", smallt[0:1, 4:5], smallt[0:1, 2:3], smallt[0:1, 3:4],
              reads=["smallt", "smallt2"], writes=["smallt3"])
            E("vector", "tensor_scalar", smallt[0:1, 4:5], smallt[0:1, 4:5], 0.7, 0.1, ALU.mult, ALU.add,
              reads=["smallt3"], writes=["smallt3"])
            E("vector", "tensor_scalar", smallt[0:1, 5:6], smallt[0:1, 2:3], 0.3, None, ALU.mult,
              reads=["smallt"], writes=["smallt3"])
            yield

        drive((gen_norm(), 6), (gen_gc(), 1))

        if s == 0:
            for c in range(CK):
                dma("sync", woTt[c][:], wT_s["wo"][c],
                    reads=["woTs0", "woTs1"], writes=[f"woTtp{c}"])

        def gen_wo(q):
            for tq in range(QTB):
                tb = q * QTB + tq
                pp = tb % 2
                xb = f"xt{tb % 2}"
                xb_t = xt[tb % 2]
                dma("sync", xb_t[:], x_d[s, tb * 128:(tb + 1) * 128, :], reads=["dx"], writes=[xb])
                for half in range(DM // 512):
                    for c in range(CK):
                        E("tensor", "matmul", ps_s[pp][:, half * 512:(half + 1) * 512],
                          attn_in[c][:, tb * 128:(tb + 1) * 128],
                          woTt[c][:, half * 512:(half + 1) * 512],
                          start=(c == 0), stop=(c == CK - 1),
                          reads=[f"attnin{c}", f"woTtp{c}"], writes=[KS[pp]])
                    yield
                E("vector", "tensor_add", xb_t[:], ps_s[pp][:, 0:DM], xb_t[:],
                  reads=[KS[pp], xb], writes=[xb])
                dma("sync", ffn_s[s, tb * 128:(tb + 1) * 128, :], xb_t[:],
                    reads=[xb], writes=[f"partD{tb % 2}"], no_waw=True)
                yield

        P.alias_barrier(VIEW_PROJ, VIEW_FFN)
        for k in range(CK):
            cast_load(T[f"w1t_{k}"][:], f"w1t_{k}", w1_d[k * 128:(k + 1) * 128, :], "dw1", DM, k % 2)
            cast_load(T[f"w2t_{k}"][:, 0:DM // 2], f"w2t_{k}", w2_d[k * 128:(k + 1) * 128, :],
                      "dw2", DM // 2, k % 2)

        def gen_ffn(q):
            qoff = q * QT
            for tq in range(QTB):
                tb = q * QTB + tq
                xb = f"xt{tq % 2}"
                xb_t = xt[tq % 2]
                dma("sync", xb_t[:], x_d[s, tb * 128:(tb + 1) * 128, :], reads=["dx"], writes=[xb])
                E("vector", "tensor_scalar", T["stream"][:, tq * DM:(tq + 1) * DM], xb_t[:],
                  rstat[:, tb:tb + 1], mr[:, tb:tb + 1], ALU.mult, ALU.add,
                  reads=[xb, "rstat", "mr"], writes=[f"stream{tq}"])
                yield
            for it in range(3):
                if it == 0:
                    def mv1(k):
                        return xn_fm[k][:, qoff:qoff + QT], xnk(qoff, QT)[0:1][0]
                    def mv1r(k):
                        return xnk(qoff, QT)
                else:
                    for tq in range(QTB):
                        sb = f"sbf{tq % 2}"
                        E("vector", "tensor_copy", T[sb][:], T["stream"][:, tq * DM:(tq + 1) * DM],
                          reads=[f"stream{tq}"], writes=[sb])
                        for c2 in range(CK // 4):
                            pw = (tq * 2 + c2) % 2
                            for cc in range(4):
                                c = c2 * 4 + cc
                                E("tensor", "matmul", ps_w[pw][:, cc * 128:(cc + 1) * 128],
                                  T[sb][:, c * 128:(c + 1) * 128], ident[:], start=True, stop=True,
                                  reads=[sb, "consts"], writes=[KW[pw]])
                            eng2 = "scalar" if pw == 0 else "vector"
                            meth2 = "copy" if pw == 0 else "tensor_copy"
                            for cc in range(4):
                                c = c2 * 4 + cc
                                E(eng2, meth2, T[f"ofm{c}"][:, tq * 128:(tq + 1) * 128],
                                  ps_w[pw][:, cc * 128:(cc + 1) * 128],
                                  reads=[KW[pw]], writes=[f"ofm{c}"])

                    def mv1(k):
                        return T[f"ofm{k}"][:, 0:QT], f"ofm{k}"
                    def mv1r(k):
                        return [f"ofm{k}"]
                for ob in range(CK):
                    pw = ob % 2
                    for k in range(CK):
                        src, _ = mv1(k)
                        E("tensor", "matmul", ps_w[pw][:, 0:QT],
                          T[f"w1t_{k}"][:, ob * 128:(ob + 1) * 128],
                          src, start=(k == 0), stop=(k == CK - 1),
                          reads=[f"w1t_{k}"] + mv1r(k), writes=[KW[pw]])
                    E("scalar", "activation", T[f"h1_{ob}"][:, 0:QT], ps_w[pw][:, 0:QT], AF.Gelu,
                      bias=b1f[:, ob:ob + 1], scale=1.0, reads=[KW[pw], "consts"], writes=[f"h1_{ob}"])
                    yield
                for ob in range(CK // 2):
                    pw = ob % 2
                    for k in range(CK):
                        E("tensor", "matmul", ps_w[pw][:, 0:QT],
                          T[f"w2t_{k}"][:, ob * 128:(ob + 1) * 128],
                          T[f"h1_{k}"][:, 0:QT], start=(k == 0), stop=(k == CK - 1),
                          reads=[f"w2t_{k}", f"h1_{k}"], writes=[KW[pw]])
                    E("scalar", "activation", T[f"h2_{ob}"][:, 0:QT], ps_w[pw][:, 0:QT], AF.Gelu,
                      bias=b2f[:, ob:ob + 1], scale=1.0, reads=[KW[pw], "consts"], writes=[f"h2_{ob}"])
                    yield
                for ob in range(CK // 4):
                    pw = ob % 2
                    for k in range(CK // 2):
                        E("tensor", "matmul", ps_w[pw][:, 0:QT],
                          w3t[k][:, ob * 128:(ob + 1) * 128],
                          T[f"h2_{k}"][:, 0:QT], start=(k == 0), stop=(k == CK // 2 - 1),
                          reads=[f"w3t{k}", f"h2_{k}"], writes=[KW[pw]])
                    E("scalar", "activation", T[f"h3_{ob}"][:, 0:QT], ps_w[pw][:, 0:QT], AF.Gelu,
                      bias=b3f[:, ob:ob + 1], scale=1.0, reads=[KW[pw], "consts"], writes=[f"h3_{ob}"])
                    yield
                for k in range(CK // 4):
                    E("tensor", "matmul", ps_w[0][0:1, 0:QT], w4t[k][:, 0:1], T[f"h3_{k}"][:, 0:QT],
                      start=(k == 0), stop=(k == CK // 4 - 1),
                      reads=[f"w4t{k}", f"h3_{k}"], writes=[KW[0]])
                E("scalar", "activation", T["cpt"][0:1, 0:QT], ps_w[0][0:1, 0:QT], AF.Tanh,
                  bias=smallt[0:1, 0:1], scale=0.5, reads=[KW[0], "smallt"], writes=["cpt"])
                E("vector", "tensor_scalar", T["cpt"][0:1, 0:QT], T["cpt"][0:1, 0:QT],
                  smallt[0:1, 5:6], smallt[0:1, 4:5], ALU.mult, ALU.add,
                  reads=["cpt", "smallt3"], writes=["cpt"])
                for tq in range(QTB):
                    E("tensor", "matmul", ps_w[1][:, tq:tq + 1],
                      T["cpt"][0:1, tq * 128:(tq + 1) * 128], onesf[0:1, 0:1],
                      start=True, stop=True, reads=["cpt", "consts"], writes=[KW[1]])
                E("vector", "tensor_copy", T["cptm"][:, 0:QTB], ps_w[1][:, 0:QTB],
                  reads=[KW[1]], writes=["cptm"])
                E("vector", "tensor_scalar", T["ncptm"][:, 0:QTB], T["cptm"][:, 0:QTB], -1.0, None,
                  ALU.mult, reads=["cptm"], writes=["ncptm"])
                for tq in range(QTB):
                    st = T["stream"]
                    o = tq * DM
                    sk2 = f"stream{tq}"
                    t01k = f"t01_{tq % 2}"
                    t01s = T["t01"][:, (tq % 2) * 256:(tq % 2) * 256 + 256]
                    t01s2 = T["t01"][:, (tq % 2) * 256 + 0:(tq % 2) * 256 + 256]
                    cp = T["cptm"][:, tq:tq + 1]
                    ncp = T["ncptm"][:, tq:tq + 1]
                    E("vector", "tensor_copy", T["t01"][:, 0:512], st[:, o:o + 512],
                      reads=[sk2], writes=["t01"]) if False else None
                    E("vector", "scalar_tensor_tensor", st[:, o + 512:o + 768], st[:, o:o + 256],
                      ncp, st[:, o + 512:o + 768], ALU.mult, ALU.add,
                      reads=[sk2, "ncptm"], writes=[sk2]) if False else None
                    E("vector", "tensor_copy", T["t01"][:, 0:512], st[:, o:o + 512],
                      reads=[sk2], writes=["t01"])
                    E("vector", "scalar_tensor_tensor", st[:, o:o + 256], st[:, o + 512:o + 768],
                      cp, st[:, o:o + 256], ALU.mult, ALU.add,
                      reads=[sk2, "cptm"], writes=[sk2])
                    E("vector", "scalar_tensor_tensor", st[:, o + 256:o + 512],
                      st[:, o + 768:o + 1024], ncp, st[:, o + 256:o + 512], ALU.mult, ALU.add,
                      reads=[sk2, "ncptm"], writes=[sk2])
                    E("vector", "scalar_tensor_tensor", st[:, o + 512:o + 768], T["t01"][:, 0:256],
                      ncp, st[:, o + 512:o + 768], ALU.mult, ALU.add,
                      reads=[sk2, "t01", "ncptm"], writes=[sk2])
                    E("vector", "scalar_tensor_tensor", st[:, o + 768:o + 1024],
                      T["t01"][:, 256:512], cp, st[:, o + 768:o + 1024], ALU.mult, ALU.add,
                      reads=[sk2, "t01", "cptm"], writes=[sk2])
                    yield
        def gen_final(q):
            for tq in range(QTB):
                tb = q * QTB + tq
                xb = f"xt{tq % 2}"
                xb_t = xt[tq % 2]
                dma("sync", xb_t[:], ffn_s[s, tb * 128:(tb + 1) * 128, :],
                    reads=["partD0", "partD1"], writes=[xb])
                E("vector", "tensor_add", T["stream"][:, tq * DM:(tq + 1) * DM],
                  T["stream"][:, tq * DM:(tq + 1) * DM], xb_t[:],
                  reads=[f"stream{tq}", xb], writes=[f"stream{tq}"])
                dma("sync", out_d[s, tb * 128:(tb + 1) * 128, :],
                    T["stream"][:, tq * DM:(tq + 1) * DM],
                    reads=[f"stream{tq}"], writes=[f"outD{tq}"], no_waw=True)
                yield

        for q in range(NQ):
            drive((gen_ffn(q), 5), (gen_wo(q), 1))
            for _ in gen_final(q):
                pass

    P.wait_only("sync", reads=[f"outD{i}" for i in range(QTB)] + (["dbgxn", "dbgattn", "dbgrows",
                                          "dbgqh", "dbgke", "dbgvp"] if dbg else []))

    P.finalize()
    with nc.Block() as block:
        P.replay(block)
    return nc


_NC_CACHE = {}


def _get_nc(S, BATCH):
    key = (S, BATCH)
    if key not in _NC_CACHE:
        _NC_CACHE[key] = build_program(S=S, BATCH=BATCH)
    return _NC_CACHE[key]


def kernel(L0_attn_x, L0_attn_wq, L0_attn_wk, L0_attn_wv, L0_attn_wo, L0_attn_pope_bias,
           L0_ffn_cn_w1, L0_ffn_cn_b1, L0_ffn_cn_w2, L0_ffn_cn_b2, L0_ffn_cn_w3, L0_ffn_cn_b3,
           L0_ffn_cn_w4, L0_ffn_cn_b4, L0_ffn_gc_w1, L0_ffn_gc_b1, L0_ffn_gc_w2, L0_ffn_gc_b2,
           L0_ffn_gc_w3, L0_ffn_gc_b3, L0_ffn_adaptive_range):
    x = np.ascontiguousarray(np.asarray(L0_attn_x, dtype=np.float32))
    B_total, S, DM = x.shape
    n_cores = 8
    bpc = B_total // n_cores
    nc = _get_nc(S, bpc)

    def f32(a):
        return np.ascontiguousarray(np.asarray(a, np.float32))

    shared = {
        "wq": f32(L0_attn_wq), "wk": f32(L0_attn_wk), "wv": f32(L0_attn_wv),
        "wo": f32(L0_attn_wo), "pope": f32(L0_attn_pope_bias),
        "cnw1": f32(L0_ffn_cn_w1), "cnb1": f32(L0_ffn_cn_b1),
        "cnw2": f32(L0_ffn_cn_w2), "cnb2": f32(L0_ffn_cn_b2),
        "cnw3": f32(L0_ffn_cn_w3), "cnb3": f32(L0_ffn_cn_b3),
        "cnw4": f32(L0_ffn_cn_w4), "cnb4": f32(L0_ffn_cn_b4),
        "gcw1": f32(L0_ffn_gc_w1), "gcb1": f32(L0_ffn_gc_b1),
        "gcw2": f32(L0_ffn_gc_w2), "gcb2": f32(L0_ffn_gc_b2),
        "gcw3": f32(L0_ffn_gc_w3), "gcb3": f32(L0_ffn_gc_b3),
        "ar": f32(L0_ffn_adaptive_range),
    }
    in_maps = []
    for c in range(n_cores):
        im = dict(shared)
        im["x"] = np.ascontiguousarray(x[c * bpc:(c + 1) * bpc])
        in_maps.append(im)
    res = run_bass_kernel_spmd(nc, in_maps, core_ids=list(range(n_cores)))
    return np.concatenate([r["out"] for r in res.results], axis=0)


# version 34
# speedup vs baseline: 1.0120x; 1.0120x over previous
import numpy as np
import ml_dtypes

import concourse.bass as bass
import concourse.mybir as mybir
from concourse.bass_utils import run_bass_kernel_spmd

F32 = mybir.dt.float32
BF16 = mybir.dt.bfloat16
AF = mybir.ActivationFunctionType
ALU = mybir.AluOpType
AX = mybir.AxisListType
PI = float(np.pi)

ENGINES = ("tensor", "scalar", "vector", "gpsimd", "sync")


class _Op:
    __slots__ = ("method", "args", "kwargs", "deps", "inc_pos", "waits", "note")

    def __init__(self, method, args, kwargs, note=""):
        self.method = method
        self.args = args
        self.kwargs = kwargs
        self.deps = []
        self.inc_pos = None
        self.waits = []
        self.note = note


class Prog:

    def __init__(self, nc):
        self.nc = nc
        self.ops = {e: [] for e in ENGINES}
        self.lastw = {}
        self.readers = {}
        self.pending = {}
        self.dma_slots = []
        self.dma_uses = []
        self.slot_for_key = {}
        self.eng_sems = {e: nc.alloc_semaphore(f"es_{e}") for e in ENGINES}
        self.slot_cons = {}

    def _slot_for(self, key):
        s = self.slot_for_key.get(key)
        if s is None:
            s = len(self.dma_slots)
            self.dma_slots.append(self.nc.alloc_semaphore(f"dsl_{key}"))
            self.dma_uses.append(0)
            self.slot_for_key[key] = s
        return s

    @staticmethod
    def _dep(deps, ref):
        if ref is not None and ref not in deps:
            deps.append(ref)

    def emit(self, eng, method, *args, reads=(), writes=(), dma=False, no_waw=False,
             note="", **kwargs):
        op = _Op(method, args, kwargs, note)
        deps = op.deps
        for k in reads:
            if k in self.pending:
                for r in self.pending.pop(k):
                    self._dep(deps, r)
            self._dep(deps, self.lastw.get(k))
        for k in writes:
            if k in self.pending:
                for r in self.pending.pop(k):
                    self._dep(deps, r)
            if not no_waw:
                self._dep(deps, self.lastw.get(k))
            for r in self.readers.get(k, ()):
                self._dep(deps, r)
        if eng == "tensor":
            op.deps = deps = [d for d in deps if not (d[0] == "E" and d[1] == "tensor")]
        data_deps = list(deps)
        if dma:
            slot = self._slot_for(writes[0])
            for r in self.slot_cons.get(slot, ()):
                self._dep(deps, r)
            self.slot_cons[slot] = []
            self.dma_uses[slot] += 1
            ref = ("D", slot, self.dma_uses[slot])
            op.inc_pos = ("D", slot)
        else:
            ref = ("E", eng, len(self.ops[eng]))
        self.ops[eng].append(op)
        for d in data_deps:
            if d[0] == "D":
                self.slot_cons.setdefault(d[1], []).append(ref)
        for k in reads:
            self.readers.setdefault(k, []).append(ref)
        for k in writes:
            self.lastw[k] = ref
            if not no_waw:
                self.readers[k] = []
        return ref

    def wait_only(self, eng, reads=(), note=""):
        self.emit(eng, None, reads=reads, note=note)

    def alias_barrier(self, old_keys, new_keys):
        refs = []
        for k in old_keys:
            w = self.lastw.get(k)
            if w is not None and w not in refs:
                refs.append(w)
            for r in self.readers.get(k, ()):
                if r not in refs:
                    refs.append(r)
        for k in new_keys:
            cur = self.pending.setdefault(k, [])
            for r in refs:
                if r not in cur:
                    cur.append(r)

    def finalize(self):
        needed = {e: set() for e in ENGINES}
        for eng in ENGINES:
            frontier = {}
            for op in self.ops[eng]:
                dmax = {}
                for ref in op.deps:
                    key = (ref[0], ref[1])
                    if dmax.get(key, -1) < ref[2]:
                        dmax[key] = ref[2]
                w = []
                for key, n in dmax.items():
                    ref = (key[0], key[1], n)
                    if frontier.get(key, -1) >= n:
                        continue
                    frontier[key] = n
                    w.append(ref)
                for ref in w:
                    if ref[0] == "E":
                        needed[ref[1]].add(ref[2])
                op.waits = w
        pos = {}
        for eng in ENGINES:
            cnt = 0
            marks = needed[eng]
            for i, op in enumerate(self.ops[eng]):
                if i in marks:
                    if op.method is None:
                        raise RuntimeError(f"pure-wait op on {eng} is a dep target")
                    cnt += 1
                    if op.inc_pos is None:
                        op.inc_pos = ("E", eng)
                    pos[(eng, i)] = cnt
        for eng in ENGINES:
            for op in self.ops[eng]:
                tw = []
                for ref in op.waits:
                    if ref[0] == "E":
                        tw.append((self.eng_sems[ref[1]], pos[(ref[1], ref[2])]))
                    else:
                        tw.append((self.dma_slots[ref[1]], 16 * ref[2]))
                op.waits = tw

    def replay(self, block):
        def runner(eng_name):
            def _run(engine):
                for op in self.ops[eng_name]:
                    for (sem, val) in op.waits:
                        engine.wait_ge(sem, val)
                    if op.method is None:
                        continue
                    inst = getattr(engine, op.method)(*op.args, **op.kwargs)
                    if op.inc_pos is not None:
                        kind, tgt = op.inc_pos
                        if kind == "D":
                            inst.then_inc(self.dma_slots[tgt], 16)
                        else:
                            inst.then_inc(self.eng_sems[tgt], 1)
            return _run

        block.tensor(runner("tensor"))
        block.scalar(runner("scalar"))
        block.vector(runner("vector"))
        block.gpsimd(runner("gpsimd"))
        block.sync(runner("sync"))


def build_program(S=2048, BATCH=2, DM=1024, NH=16, dbg=False):
    D = DM // NH
    CK = DM // 128
    NP = NH // 2
    TB = S // 128
    SKB = S // 128
    SQC = min(1024, S)
    NSQ = S // SQC
    NQ = 4 if TB >= 4 else TB
    QTB = TB // NQ
    QT = QTB * 128
    MMN = min(512, S)
    NCH = S // MMN
    EXP_MARGIN = 2.0

    nc = bass.Bass("TRN2", target_bir_lowering=False)

    x_d = nc.dram_tensor("x", [BATCH, S, DM], F32, kind="ExternalInput").ap()
    wq_d = nc.dram_tensor("wq", [DM, DM], F32, kind="ExternalInput").ap()
    wk_d = nc.dram_tensor("wk", [DM, DM], F32, kind="ExternalInput").ap()
    wv_d = nc.dram_tensor("wv", [DM, DM], F32, kind="ExternalInput").ap()
    wo_d = nc.dram_tensor("wo", [DM, DM], F32, kind="ExternalInput").ap()
    pope_d = nc.dram_tensor("pope", [NH, D], F32, kind="ExternalInput").ap()
    w1_d = nc.dram_tensor("cnw1", [DM, DM], F32, kind="ExternalInput").ap()
    b1_d = nc.dram_tensor("cnb1", [DM], F32, kind="ExternalInput").ap()
    w2_d = nc.dram_tensor("cnw2", [DM, DM // 2], F32, kind="ExternalInput").ap()
    b2_d = nc.dram_tensor("cnb2", [DM // 2], F32, kind="ExternalInput").ap()
    w3_d = nc.dram_tensor("cnw3", [DM // 2, DM // 4], F32, kind="ExternalInput").ap()
    b3_d = nc.dram_tensor("cnb3", [DM // 4], F32, kind="ExternalInput").ap()
    w4_d = nc.dram_tensor("cnw4", [DM // 4, 1], F32, kind="ExternalInput").ap()
    b4_d = nc.dram_tensor("cnb4", [1], F32, kind="ExternalInput").ap()
    g1_d = nc.dram_tensor("gcw1", [DM, DM // 2], F32, kind="ExternalInput").ap()
    gb1_d = nc.dram_tensor("gcb1", [DM // 2], F32, kind="ExternalInput").ap()
    g2_d = nc.dram_tensor("gcw2", [DM // 2, DM // 4], F32, kind="ExternalInput").ap()
    gb2_d = nc.dram_tensor("gcb2", [DM // 4], F32, kind="ExternalInput").ap()
    g3_d = nc.dram_tensor("gcw3", [DM // 4, 1], F32, kind="ExternalInput").ap()
    gb3_d = nc.dram_tensor("gcb3", [1], F32, kind="ExternalInput").ap()
    ar_d = nc.dram_tensor("ar", [1], F32, kind="ExternalInput").ap()
    out_d = nc.dram_tensor("out", [BATCH, S, DM], F32, kind="ExternalOutput").ap()
    wT_s = {w: nc.dram_tensor(f"{w}T_s", [CK, 128, DM], BF16, kind="Internal").ap()
            for w in ("wq", "wk", "wv", "wo")}
    ffn_s = nc.dram_tensor("ffn_s", [BATCH, S, DM], F32, kind="Internal").ap()
    if dbg:
        dbg_xn = nc.dram_tensor("dbg_xn", [CK, 128, S], BF16, kind="ExternalOutput").ap()
        dbg_qh = nc.dram_tensor("dbg_qh", [2, 128, S], BF16, kind="ExternalOutput").ap()
        dbg_ke = nc.dram_tensor("dbg_ke", [2, 128, S], BF16, kind="ExternalOutput").ap()
        dbg_vp = nc.dram_tensor("dbg_vp", [128, (S // 128) * 130], BF16, kind="ExternalOutput").ap()
        dbg_rows = nc.dram_tensor("dbg_rows", [16, S], F32, kind="ExternalOutput").ap()
        dbg_expb = nc.dram_tensor("dbg_expb", [128, NH], F32, kind="ExternalOutput").ap()
        dbg_attn = nc.dram_tensor("dbg_attn", [CK, 128, S], BF16, kind="ExternalOutput").ap()

    inv = 10000.0 ** (-(np.arange(D, dtype=np.float64) / D))
    fr = np.arange(S, dtype=np.float64)[None, :] * inv[:, None]
    t1_np = np.concatenate([np.cos(fr), np.sin(fr)], 0).astype(ml_dtypes.bfloat16)
    t1s_np = np.concatenate([np.sin(fr), np.cos(fr)], 0).astype(ml_dtypes.bfloat16)
    T1_d = nc.inline_tensor(t1_np, "T1c").ap()
    T1s_d = nc.inline_tensor(t1s_np, "T1sc").ap()
    ident_d = nc.inline_tensor(np.eye(128, dtype=ml_dtypes.bfloat16), "identc").ap()
    i16_d = nc.inline_tensor(np.eye(16, dtype=np.float32), "i16c").ap()
    onesf_d = nc.inline_tensor(np.ones((1, 128), dtype=np.float32), "onesfc").ap()
    o2 = np.zeros((128, 2), dtype=ml_dtypes.bfloat16)
    o2[0:64, 0] = 1.0
    o2[64:128, 1] = 1.0
    ones2_d = nc.inline_tensor(o2, "ones2c").ap()

    A = nc.alloc_sbuf_tensor
    cpool = A("cpool", [128, 5], F32)
    T1 = A("T1", [128, S], BF16)
    T1s = A("T1s", [128, S], BF16)
    ident = A("ident", [128, 128], BF16)
    i16 = A("i16", [16, 16], F32)
    onesf = A("onesf", [1, 128], F32)
    ones2 = A("ones2", [128, 2], BF16)
    pbt = A("pbt", [16, D], F32)
    bcl = A("bcl", [16, D], F32)
    btr = A("btr", [D, 16], F32)
    sbv = A("sbv", [D, 16], F32)
    cbv = A("cbv", [D, 16], F32)
    ttv = A("ttv", [D, 16], F32)
    cbdup = A("cbdup", [128, NH], F32)
    sbdup = A("sbdup", [128, NH], F32)
    mstat = A("mstat", [128, TB], F32)
    rstat = A("rstat", [128, TB], F32)
    mr = A("mr", [128, TB], F32)
    rowsums = A("rowsums", [16, S], F32)
    rcps = A("rcps", [1, SQC], F32)
    gcv = A("gcv", [128, CK], F32)
    gcvb = A("gcvb", [128, CK], BF16)
    b1f = A("b1f", [128, CK], F32)
    b2f = A("b2f", [128, CK // 2], F32)
    b3f = A("b3f", [128, CK // 4], F32)
    gbf1 = A("gbf1", [128, CK // 2], F32)
    gbf2 = A("gbf2", [128, CK // 4], F32)
    smallt = A("smallt", [1, 8], F32)
    tmpB = [A(f"tmpB{i}", [64, SQC], BF16) for i in range(2)]
    rowstg0 = A("rowstg0", [65, SQC], F32)
    woTt = [A(f"woTtp{c}", [128, DM], BF16) for c in range(CK)]
    w3t = [A(f"w3t{k}", [128, DM // 4], BF16) for k in range(4)]
    w4t = [A(f"w4t{k}", [128, 1], BF16) for k in range(2)]
    gw2t = [A(f"gw2t{k}", [128, DM // 4], BF16) for k in range(4)]
    gw3t = [A(f"gw3t{k}", [128, 1], BF16) for k in range(2)]
    hg1 = A("hg1", [128, 4], BF16)
    hg2 = A("hg2", [128, 2], BF16)
    xn_fm = [A(f"xnfm{c}", [128, S], BF16) for c in range(CK)]
    attn_in = [A(f"attnin{c}", [128, S], BF16) for c in range(CK)]
    xt = [A(f"xt{i}", [128, DM], F32) for i in range(2)]
    xsq = A("xsq", [128, DM], F32)
    v1t = [A(f"v1t{i}", [128, 1], F32) for i in range(2)]

    def rnd(x):
        return (x + 63) // 64 * 64

    lay = {}

    def place(name, shape, dtype, cursor):
        nbytes = shape[1] * (2 if dtype == BF16 else 4)
        lay[name] = (shape, dtype, cursor)
        return cursor + rnd(nbytes)

    arena_sz = 0
    cur = 0
    cur = place("qmag", [128, S], BF16, cur)
    cur = place("kmag", [128, S], BF16, cur)
    cur = place("tmp1", [128, S], BF16, cur)
    cur = place("tmpe", [128, max(MMN, DM)], F32, cur)
    for b in range(2):
        for h in range(2):
            cur = place(f"qh{b}_{h}", [128, S], BF16, cur)
            cur = place(f"ke{b}_{h}", [128, S], BF16, cur)
    for b in range(2):
        cur = place(f"vp{b}", [128, SKB * 130], BF16, cur)
    for b in range(3):
        cur = place(f"pt{b}", [128, SQC], BF16, cur)
    for b in range(2):
        for w in ("wq", "wk", "wv"):
            cur = place(f"{w}t{b}", [128, DM], BF16, cur)
    cur = place("tmpv0", [128, MMN], BF16, cur)
    cur = place("tmpv1", [128, MMN], BF16, cur)
    cur = place("xnb0", [128, DM], BF16, cur)
    cur = place("xnb1", [128, DM], BF16, cur)
    arena_sz = max(arena_sz, cur)
    VIEW_PROJ = ["qmag", "kmag", "tmp1", "tmpe", "tmpv0", "tmpv1",
                 "xnb0", "xnb1", "vp0", "vp1", "pt0", "pt1", "pt2"] \
        + [f"qh{b}_{h}" for b in range(2) for h in range(2)] \
        + [f"ke{b}_{h}" for b in range(2) for h in range(2)] \
        + [f"{w}t{b}" for b in range(2) for w in ("wq", "wk", "wv")]

    cur = 0
    cur = place("stream", [128, QTB * DM], F32, cur)
    for c in range(CK):
        cur = place(f"ofm{c}", [128, QT], BF16, cur)
    cur = place("sbf0", [128, DM], BF16, cur)
    cur = place("sbf1", [128, DM], BF16, cur)
    for c in range(CK):
        cur = place(f"h1_{c}", [128, QT], BF16, cur)
    for c in range(CK // 2):
        cur = place(f"h2_{c}", [128, QT], BF16, cur)
    for c in range(CK // 4):
        cur = place(f"h3_{c}", [128, QT], BF16, cur)
    cur = place("t01", [128, 512], F32, cur)
    cur = place("cpt", [1, QT], F32, cur)
    cur = place("cptm", [128, QTB], F32, cur)
    cur = place("ncptm", [128, QTB], F32, cur)
    for k in range(CK):
        cur = place(f"w1t_{k}", [128, DM], BF16, cur)
        cur = place(f"w2t_{k}", [128, DM // 2], BF16, cur)
        cur = place(f"gw1t_{k}", [128, DM // 2], BF16, cur)
    arena_sz = max(arena_sz, cur)
    VIEW_FFN = ([f"stream{i}" for i in range(QTB)]
                + ["sbf0", "sbf1", "t01", "cpt", "cptm", "ncptm"]
                + [f"ofm{c}" for c in range(CK)]
                + [f"h1_{c}" for c in range(CK)]
                + [f"h2_{c}" for c in range(CK // 2)]
                + [f"h3_{c}" for c in range(CK // 4)]
                + [f"w1t_{k}" for k in range(CK)]
                + [f"w2t_{k}" for k in range(CK)]
                + [f"gw1t_{k}" for k in range(CK)])

    VIEW_WO = []

    base, _ = nc.bump_sbuf(arena_sz)
    T = {}
    for name, (shape, dtype, off) in lay.items():
        T[name] = nc.alloc_sbuf_tensor_at(name, shape, dtype, offset=base + off)
    rcpsB = nc.alloc_sbuf_tensor_at("rcpsB", [1, SQC], F32, offset=base + lay["pt0"][2])

    ps_s = [nc.alloc_psum_tensor("ps_s0", [128, 1024], F32),
            nc.alloc_psum_tensor("ps_s1", [128, 1024], F32)]
    ps_pv = nc.alloc_psum_tensor("ps_pv", [65, 1024], F32)
    ps_w = [nc.alloc_psum_tensor("ps_w0", [128, 512], F32),
            nc.alloc_psum_tensor("ps_w1", [128, 512], F32)]
    KW = ["ps_w0", "ps_w1"]
    KS = ["ps_s0", "ps_s1"]

    P = Prog(nc)
    E = P.emit

    def dma(queue, dst, src, reads, writes, note="", no_waw=False, **kw):
        return E(queue, "dma_start", dst, src, reads=reads, writes=writes, dma=True,
                 no_waw=no_waw, note=note, **kw)

    E("vector", "memset", cpool[:, 0:1], 1e-5, writes=["cpool"])
    E("vector", "memset", cpool[:, 1:2], 0.5 * PI, reads=["cpool"], writes=["cpool"])
    E("vector", "memset", cpool[:, 2:3], PI, reads=["cpool"], writes=["cpool"])
    E("vector", "memset", cpool[:, 3:4], 1.0, reads=["cpool"], writes=["cpool"])
    E("vector", "memset", cpool[:, 4:5], -8.0, reads=["cpool"], writes=["cpool"])
    dma("sync", T1[:], T1_d, reads=["dT1"], writes=["consts"], no_waw=True)
    dma("sync", T1s[:], T1s_d, reads=["dT1s"], writes=["consts"], no_waw=True)
    dma("sync", ident[:], ident_d, reads=["did"], writes=["consts"], no_waw=True)
    dma("sync", i16[:], i16_d, reads=["di16"], writes=["consts"], no_waw=True)
    dma("sync", onesf[:], onesf_d, reads=["donesf"], writes=["consts"], no_waw=True)
    dma("sync", ones2[:], ones2_d, reads=["dones2"], writes=["consts"], no_waw=True)
    dma("sync", pbt[:], pope_d, reads=["dpope"], writes=["pbt"])
    dma("sync", b1f[:], b1_d.rearrange("(j p) -> p j", p=128), reads=["db1"], writes=["consts"], no_waw=True, allow_slow_non_contiguous=True)
    dma("sync", b2f[:], b2_d.rearrange("(j p) -> p j", p=128), reads=["db2"], writes=["consts"], no_waw=True, allow_slow_non_contiguous=True)
    dma("sync", b3f[:], b3_d.rearrange("(j p) -> p j", p=128), reads=["db3"], writes=["consts"], no_waw=True, allow_slow_non_contiguous=True)
    dma("sync", gbf1[:], gb1_d.rearrange("(j p) -> p j", p=128), reads=["dgb1"], writes=["consts"], no_waw=True, allow_slow_non_contiguous=True)
    dma("sync", gbf2[:], gb2_d.rearrange("(j p) -> p j", p=128), reads=["dgb2"], writes=["consts"], no_waw=True, allow_slow_non_contiguous=True)
    dma("sync", smallt[0:1, 0:1], b4_d[None, :], reads=["db4"], writes=["smallt"])
    dma("sync", smallt[0:1, 1:2], gb3_d[None, :], reads=["dgb3"], writes=["smallt"])
    dma("sync", smallt[0:1, 2:3], ar_d[None, :], reads=["dar"], writes=["smallt"])
    E("vector", "tensor_scalar", smallt[0:1, 0:2], smallt[0:1, 0:2], 0.5, None, ALU.mult,
      reads=["smallt"], writes=["smallt"])
    def cast_load(dst_ap, dst_key, src_ap, src_key, width, stg_i):
        dma("gpsimd", dst_ap, src_ap, reads=[src_key], writes=[dst_key])

    for k in range(4):
        cast_load(w3t[k][:], f"w3t{k}", w3_d[k * 128:(k + 1) * 128, :], "dw3", DM // 4, k % 2)
        cast_load(gw2t[k][:], f"gw2t{k}", g2_d[k * 128:(k + 1) * 128, :], "dg2", DM // 4, k % 2)
    for k in range(2):
        cast_load(w4t[k][:], f"w4t{k}", w4_d[k * 128:(k + 1) * 128, :], "dw4", 1, k % 2)
        cast_load(gw3t[k][:], f"gw3t{k}", g3_d[k * 128:(k + 1) * 128, :], "dg3", 1, k % 2)

    E("vector", "tensor_scalar", bcl[:], pbt[:], 0.0, -2 * PI, ALU.min, ALU.max,
      reads=["pbt"], writes=["bcl"])
    E("tensor", "matmul", ps_w[0][0:D, 0:16], bcl[:], i16[:], start=True, stop=True,
      reads=["bcl", "consts"], writes=[KW[0]])
    E("vector", "tensor_copy", btr[:], ps_w[0][0:D, 0:16], reads=[KW[0]], writes=["btr"])
    E("scalar", "activation", sbv[:], btr[:], AF.Sin, bias=cpool[0:D, 2:3], scale=1.0,
      reads=["btr", "cpool"], writes=["sbv"])
    E("scalar", "activation", ttv[:], btr[:], AF.Abs, bias=cpool[0:D, 2:3], scale=1.0,
      reads=["btr", "cpool"], writes=["ttv"])
    E("scalar", "activation", cbv[:], ttv[:], AF.Sin, bias=cpool[0:D, 1:2], scale=-1.0,
      reads=["ttv", "cpool"], writes=["cbv"])
    E("vector", "tensor_scalar", cbv[:], cbv[:], -1.0, None, ALU.mult, reads=["cbv"], writes=["cbv"])
    E("vector", "tensor_scalar", sbv[:], sbv[:], -1.0, None, ALU.mult, reads=["sbv"], writes=["sbv"])
    dma("sync", cbdup[0:64, :], cbv[:], reads=["cbv"], writes=["cbdup"])
    dma("sync", cbdup[64:128, :], cbv[:], reads=["cbv"], writes=["cbdup"])
    dma("sync", sbdup[0:64, :], sbv[:], reads=["sbv"], writes=["sbdup"])
    E("vector", "tensor_scalar", ttv[:], sbv[:], -1.0, None, ALU.mult, reads=["sbv"], writes=["ttv"])
    dma("sync", sbdup[64:128, :], ttv[:], reads=["ttv"], writes=["sbdup"])

    if S * 2 >= 4 * DM:
        prepCb = nc.alloc_sbuf_tensor_at("prepCb", [128, DM], BF16, offset=base + lay["qh1_0"][2])
        prepDb = nc.alloc_sbuf_tensor_at("prepDb", [128, DM], BF16, offset=base + lay["qh1_1"][2])
    else:
        prepCb = prepDb = None

    def gen_prep(olap):
        names = {"wq": wq_d, "wk": wk_d, "wv": wv_d, "wo": wo_d}
        if olap:
            order = [(w, i) for i in range(CK) for w in ("wq", "wk", "wv")] \
                + [("wo", i) for i in range(CK)]
        else:
            order = [(w, i) for w in ("wq", "wk", "wv", "wo") for i in range(CK)]
        if olap:
            bufs = [(T["wqt1"], "wqt1"), (T["wkt1"], "wkt1"),
                    (prepCb, "qh1_0"), (prepDb, "qh1_1")]
            nb = 4
            stg4 = [(T["ke1_0"][:, 0:DM], ["stgA", "ke1_0"]),
                    (T["ke1_1"][:, 0:DM], ["stgB", "ke1_1"]),
                    (T["ke1_0"][:, DM:2 * DM], ["stgC", "ke1_0"]),
                    (T["ke1_1"][:, DM:2 * DM], ["stgD", "ke1_1"])]
            ns = 4
        elif S >= DM:
            bufs = [(T["qh0_0"], "qh0_0"), (T["qh0_1"], "qh0_1")]
            nb = 2
        else:
            bufs = [(T["xnb0"], "xnb0"), (T["xnb1"], "xnb1")]
            nb = 2
        for t, (wname, i) in enumerate(order):
            wd = names[wname]
            if True:
                wtmp_ap, wtmp = bufs[t % nb]
                if olap:
                    stg_ap, stg_keys = stg4[t % ns]
                else:
                    stg = "wqt0" if t % 2 == 0 else "wqt1"
                    stg_ap, stg_keys = T[stg][:, 0:DM], [stg]
                dma("gpsimd", wtmp_ap[:, 0:DM], wd[i * 128:(i + 1) * 128, :],
                    reads=[f"d{wname}"], writes=[f"prepc{t % nb}", wtmp])
                yield
                for j2 in range(CK // 4):
                    pw = j2 % 2
                    for jj in range(4):
                        j = j2 * 4 + jj
                        E("tensor", "matmul", ps_w[pw][:, jj * 128:(jj + 1) * 128],
                          wtmp_ap[:, j * 128:(j + 1) * 128], ident[:], start=True, stop=True,
                          reads=[wtmp, "consts"], writes=[KW[pw]])
                    E("vector", "tensor_copy", stg_ap[:, j2 * 512:(j2 + 1) * 512], ps_w[pw][:, :],
                      reads=[KW[pw]], writes=stg_keys, no_waw=len(stg_keys) > 1)
                    yield
                if wname == "wo":
                    for j in range(CK):
                        dma("sync", wT_s[wname][j][:, i * 128:(i + 1) * 128],
                            stg_ap[:, j * 128:(j + 1) * 128],
                            reads=stg_keys, writes=[f"{wname}Ts{i % 2}"], no_waw=True)
                else:
                    dma("sync", wT_s[wname][i], stg_ap[:, 0:DM], reads=stg_keys,
                        writes=[f"{wname}Ts{i % 2}"], no_waw=True)
                yield

    def xnk(tok_lo, n_tok):
        g_lo, g_hi = tok_lo // MMN, (tok_lo + n_tok - 1) // MMN
        return [f"xnfm{c}g{g}" for c in range(CK) for g in range(g_lo, g_hi + 1)]

    def drive(*pairs):
        gens = [g for g, n in pairs]
        rates = [n for g, n in pairs]
        done = [False] * len(gens)
        while not all(done):
            for gi in range(len(gens)):
                for _ in range(rates[gi]):
                    if not done[gi]:
                        done[gi] = next(gens[gi], _SENT) is _SENT
    _SENT = object()

    for s in range(BATCH):
        P.alias_barrier(VIEW_FFN, VIEW_PROJ)

        def gen_ln():
            for tb in range(TB):
                xb = f"xt{tb % 2}"
                xb_t = xt[tb % 2]
                dma("sync", xb_t[:], x_d[s, tb * 128:(tb + 1) * 128, :], reads=["dx"], writes=[xb])
                E("vector", "reduce_sum", mstat[:, tb:tb + 1], xb_t[:], axis=AX.X,
                  reads=[xb], writes=["mstat"])
                E("scalar", "activation", xsq[:], xb_t[:], AF.Square, bias=0.0, scale=1.0,
                  reads=[xb], writes=["xsq"])
                E("vector", "reduce_sum", rstat[:, tb:tb + 1], xsq[:], axis=AX.X,
                  reads=["xsq"], writes=["rstat"])
                yield
            E("vector", "tensor_scalar", mstat[:, :], mstat[:, :], 1.0 / DM, None, ALU.mult,
              reads=["mstat"], writes=["mstat"])
            E("vector", "tensor_mul", mr[:, :], mstat[:, :], mstat[:, :],
              reads=["mstat"], writes=["mr"])
            E("vector", "tensor_scalar", rstat[:, :], rstat[:, :], 1.0 / DM, None, ALU.mult,
              reads=["rstat"], writes=["rstat"])
            E("vector", "tensor_sub", rstat[:, :], rstat[:, :], mr[:, :],
              reads=["rstat", "mr"], writes=["rstat"])
            E("scalar", "activation", rstat[:, :], rstat[:, :], AF.Ln, bias=cpool[:, 0:1],
              scale=1.0, reads=["rstat", "cpool"], writes=["rstat"])
            E("scalar", "activation", rstat[:, :], rstat[:, :], AF.Exp, bias=0.0, scale=-0.5,
              reads=["rstat"], writes=["rstat"])
            E("vector", "scalar_tensor_tensor", mr[:, :], mstat[:, :], -1.0, rstat[:, :],
              ALU.mult, ALU.mult, reads=["mstat", "rstat"], writes=["mr"])
            yield
            for tb in range(TB):
                xb = f"xt{tb % 2}"
                xb_t = xt[tb % 2]
                dma("sync", xb_t[:], x_d[s, tb * 128:(tb + 1) * 128, :], reads=["dx"], writes=[xb])
                xnb = "xnb0" if tb % 2 == 0 else "xnb1"
                E("vector", "tensor_scalar", T[xnb][:], xb_t[:], mstat[:, tb:tb + 1],
                  rstat[:, tb:tb + 1], ALU.subtract, ALU.mult,
                  reads=[xb, "mstat", "rstat"], writes=[xnb])
                yield
                for c2 in range(CK // 4):
                    pw = (tb * 2 + c2) % 2
                    for cc in range(4):
                        c = c2 * 4 + cc
                        E("tensor", "matmul", ps_w[pw][:, cc * 128:(cc + 1) * 128],
                          T[xnb][:, c * 128:(c + 1) * 128], ident[:], start=True, stop=True,
                          reads=[xnb, "consts"], writes=[KW[pw]])
                    eng2 = "scalar" if pw == 0 else "vector"
                    meth = "copy" if eng2 == "scalar" else "tensor_copy"
                    for cc in range(4):
                        c = c2 * 4 + cc
                        E(eng2, meth, xn_fm[c][:, tb * 128:(tb + 1) * 128],
                          ps_w[pw][:, cc * 128:(cc + 1) * 128],
                          reads=[KW[pw]], writes=[f"xnfm{c}g{tb * 128 // MMN}"])
                yield

        def gen_projembed(j, midyield=True):
            B = j % 2
            hA = 2 * j
            for wname in ("wq", "wk", "wv"):
                dma("sync", T[f"{wname}t{B}"][:], wT_s[wname][j],
                    reads=[f"{wname}Ts0", f"{wname}Ts1"], writes=[f"{wname}t{B}"])
            yield
            for wname, magk in (("wq", "qmag"), ("wk", "kmag")):
                for ch in range(NCH):
                    pw = ch % 2
                    for c in range(CK):
                        E("tensor", "matmul", ps_w[pw][:, 0:MMN],
                          T[f"{wname}t{B}"][:, c * 128:(c + 1) * 128],
                          xn_fm[c][:, ch * MMN:(ch + 1) * MMN],
                          start=(c == 0), stop=(c == CK - 1),
                          reads=[f"{wname}t{B}"] + xnk(ch * MMN, MMN), writes=[KW[pw]])
                        if c == 3 and midyield:
                            yield
                    E("scalar", "copy", T[magk][:, ch * MMN:(ch + 1) * MMN],
                      ps_w[pw][:, 0:MMN], reads=[KW[pw]], writes=[magk])
                    yield
                SPW = min(S, 1024)
                for o_ in range(0, S, SPW):
                    E("scalar", "activation", T["tmpe"][:, 0:SPW], T[magk][:, o_:o_ + SPW],
                      AF.Exp, bias=0.0, scale=1.0, reads=[magk], writes=["tmpe"])
                    E("scalar", "activation", T[magk][:, o_:o_ + SPW], T["tmpe"][:, 0:SPW],
                      AF.Ln, bias=cpool[:, 3:4], scale=1.0, reads=["tmpe", "cpool"], writes=[magk])
                    yield
            for ch in range(NCH):
                pw = ch % 2
                for c in range(CK):
                    E("tensor", "matmul", ps_w[pw][:, 0:MMN],
                      T[f"wvt{B}"][:, c * 128:(c + 1) * 128],
                      xn_fm[c][:, ch * MMN:(ch + 1) * MMN],
                      start=(c == 0), stop=(c == CK - 1),
                      reads=[f"wvt{B}"] + xnk(ch * MMN, MMN), writes=[KW[pw]])
                    if c == 3 and midyield:
                        yield
                tv = f"tmpv{pw}"
                E("vector", "tensor_copy", T[tv][:, 0:MMN], ps_w[pw][:, 0:MMN],
                  reads=[KW[pw]], writes=[tv])
                yield
                if True:
                    pass
                for bb in range(MMN // 128):
                    blk = ch * (MMN // 128) + bb
                    pv2 = (ch + 1) % 2
                    E("tensor", "matmul", ps_w[pv2][:, (bb % 4) * 128:(bb % 4 + 1) * 128],
                      T[tv][:, bb * 128:(bb + 1) * 128], ident[:], start=True, stop=True,
                      reads=[tv, "consts"], writes=[KW[pv2]])
                    E("vector", "tensor_copy", T[f"vp{B}"][:, blk * 130:blk * 130 + 64],
                      ps_w[pv2][:, (bb % 4) * 128:(bb % 4) * 128 + 64],
                      reads=[KW[pv2]], writes=[f"vp{B}"])
                    E("vector", "tensor_copy", T[f"vp{B}"][:, blk * 130 + 65:blk * 130 + 129],
                      ps_w[pv2][:, (bb % 4) * 128 + 64:(bb % 4) * 128 + 128],
                      reads=[KW[pv2]], writes=[f"vp{B}"])
                yield
            vpr = T[f"vp{B}"].ap().rearrange("p (b c) -> p b c", c=130)
            E("vector", "memset", vpr[:, :, 64:65], 1.0, reads=[f"vp{B}"], writes=[f"vp{B}"])
            E("vector", "memset", vpr[:, :, 129:130], 1.0, reads=[f"vp{B}"], writes=[f"vp{B}"])
            yield
            for hh in range(2):
                h = 2 * j + hh
                kek = f"ke{B}_{hh}"
                ket = T[kek]
                qhk = f"qh{B}_{hh}"
                qht = T[qhk]
                if hh == 0:
                    E("vector", "tensor_mul", ket[0:64, :], T["kmag"][0:64, :], T1[0:64, :],
                      reads=["kmag", "consts"], writes=[kek])
                    dma("gpsimd", ket[64:128, :], T["kmag"][0:64, :], reads=["kmag"], writes=[kek])
                    E("vector", "tensor_mul", ket[64:128, :], ket[64:128, :], T1[64:128, :],
                      reads=[kek, "consts"], writes=[kek])
                    yield
                    E("vector", "tensor_mul", T["tmp1"][0:64, :], T["qmag"][0:64, :], T1[0:64, :],
                      reads=["qmag", "consts"], writes=["tmp1"])
                    E("vector", "tensor_mul", qht[0:64, :], T["qmag"][0:64, :], T1s[0:64, :],
                      reads=["qmag", "consts"], writes=[qhk])
                    dma("gpsimd", qht[64:128, :], T["qmag"][0:64, :], reads=["qmag"], writes=[qhk])
                    dma("gpsimd", T["tmp1"][64:128, :], T["qmag"][0:64, :], reads=["qmag"], writes=["tmp1"])
                    yield
                    E("vector", "tensor_mul", T["tmp1"][64:128, :], T["tmp1"][64:128, :], T1[64:128, :],
                      reads=["tmp1", "consts"], writes=["tmp1"])
                    E("vector", "tensor_mul", qht[64:128, :], qht[64:128, :], T1s[64:128, :],
                      reads=[qhk, "consts"], writes=[qhk])
                    yield
                else:
                    dma("gpsimd", ket[0:64, :], T["kmag"][64:128, :], reads=["kmag"], writes=[kek])
                    E("vector", "tensor_mul", ket[0:64, :], ket[0:64, :], T1[0:64, :],
                      reads=[kek, "consts"], writes=[kek])
                    E("vector", "tensor_mul", ket[64:128, :], T["kmag"][64:128, :], T1[64:128, :],
                      reads=["kmag", "consts"], writes=[kek])
                    yield
                    dma("gpsimd", qht[0:64, :], T["qmag"][64:128, :], reads=["qmag"], writes=[qhk])
                    dma("gpsimd", T["tmp1"][0:64, :], T["qmag"][64:128, :], reads=["qmag"], writes=["tmp1"])
                    E("vector", "tensor_mul", T["tmp1"][0:64, :], T["tmp1"][0:64, :], T1[0:64, :],
                      reads=["tmp1", "consts"], writes=["tmp1"])
                    E("vector", "tensor_mul", qht[0:64, :], qht[0:64, :], T1s[0:64, :],
                      reads=[qhk, "consts"], writes=[qhk])
                    yield
                    E("vector", "tensor_mul", T["tmp1"][64:128, :], T["qmag"][64:128, :], T1[64:128, :],
                      reads=["qmag", "consts"], writes=["tmp1"])
                    E("vector", "tensor_mul", qht[64:128, :], T["qmag"][64:128, :], T1s[64:128, :],
                      reads=["qmag", "consts"], writes=[qhk])
                    yield
                E("vector", "tensor_scalar", T["tmp1"][:, :], T["tmp1"][:, :], cbdup[:, h:h + 1],
                  None, ALU.mult, reads=["tmp1", "cbdup"], writes=["tmp1"])
                E("vector", "scalar_tensor_tensor", qht[:, :], qht[:, :], sbdup[:, h:h + 1],
                  T["tmp1"][:, :], ALU.mult, ALU.add, reads=[qhk, "tmp1", "sbdup"], writes=[qhk])
                yield
            yield

        if s == 0:
            if S * 2 >= 4 * DM:
                gl, gp = gen_ln(), gen_prep(True)
                ln_done = False
                while not ln_done:
                    for _ in range(2):
                        if not ln_done:
                            ln_done = next(gl, _SENT) is _SENT
                    next(gp, None)
                drive((gp, 3), (gen_projembed(0, midyield=False), 1))
            else:
                for _ in gen_prep(False):
                    pass
                for _ in gen_ln():
                    pass
        else:
            for _ in gen_ln():
                pass

        for c in range(CK):
            E("vector", "reduce_sum", gcv[:, c:c + 1], xn_fm[c][:, :], axis=AX.X,
              reads=xnk(0, S), writes=["gcv"])

        def emit_scores(jp, hh, sq, blk, pp):
            B = jp % 2
            qhk, kek = f"qh{B}_{hh}", f"ke{B}_{hh}"
            for m in range(SQC // MMN):
                E("tensor", "matmul", ps_s[pp][:, m * MMN:(m + 1) * MMN],
                  T[kek][:, blk * 128:(blk + 1) * 128],
                  T[qhk][:, sq * SQC + m * MMN: sq * SQC + (m + 1) * MMN],
                  start=True, stop=True, reads=[kek, qhk], writes=[KS[pp]])

        def emit_exp(pp, ptb):
            E("scalar", "activation", T[f"pt{ptb}"][:, 0:SQC], ps_s[pp][:, 0:SQC],
              AF.Exp, bias=cpool[:, 4:5], scale=0.125,
              reads=[KS[pp], "cpool"], writes=[f"pt{ptb}"])

        def emit_pv(jp, hh, sq, blk, ptb):
            B = jp % 2
            for m in range(SQC // MMN):
                E("tensor", "matmul", ps_pv[:, m * MMN:(m + 1) * MMN],
                  T[f"vp{B}"][:, blk * 130 + hh * 65: blk * 130 + hh * 65 + 65],
                  T[f"pt{ptb}"][:, m * MMN:(m + 1) * MMN],
                  start=(blk == 0), stop=(blk == SKB - 1),
                  reads=[f"vp{B}", f"pt{ptb}"], writes=["ps_pv"])

        def emit_evac(jp, hh, sq):
            h = 2 * jp + hh
            rsi = (h * NSQ + sq) % 2
            if rsi == 0:
                rs_ap, rsk = rowstg0[64:65, 0:SQC], "rowstg0"
            else:
                rs_ap, rsk = xsq[64:65, 0:SQC], "xsq"
            E("vector", "tensor_copy", rs_ap, ps_pv[64:65, 0:SQC],
              reads=["ps_pv"], writes=[rsk])
            dma("sync", rowsums[h:h + 1, sq * SQC:(sq + 1) * SQC],
                rs_ap, reads=[rsk], writes=["rowsums"])
            if hh == 0:
                E("vector", "tensor_copy", attn_in[jp][0:64, sq * SQC:(sq + 1) * SQC],
                  ps_pv[0:64, 0:SQC], reads=["ps_pv"], writes=[f"attnin{jp}"])
            else:
                tBk = f"tmpB{sq % 2}"
                tBt = tmpB[sq % 2]
                E("vector", "tensor_copy", tBt[:, 0:SQC], ps_pv[0:64, 0:SQC],
                  reads=["ps_pv"], writes=[tBk])
                dma("sync", attn_in[jp][64:128, sq * SQC:(sq + 1) * SQC], tBt[:, 0:SQC],
                    reads=[tBk], writes=[f"attnin{jp}"])

        att_pend = []

        def gen_attention(j):
            steps = [(hh, sq, blk) for hh in range(2) for sq in range(NSQ)
                     for blk in range(SKB)]
            nsp = len(steps)
            for i, (hh, sq, blk) in enumerate(steps):
                gi = j * nsp + i
                pp = gi % 2
                ptb = gi % 3
                emit_scores(j, hh, sq, blk, pp)
                emit_exp(pp, ptb)
                if len(att_pend) == 2:
                    oj, ohh, osq, oblk, optb = att_pend.pop(0)
                    emit_pv(oj, ohh, osq, oblk, optb)
                    if oblk == SKB - 1:
                        emit_evac(oj, ohh, osq)
                att_pend.append((j, hh, sq, blk, ptb))
                yield
            if j == NP - 1:
                while att_pend:
                    oj, ohh, osq, oblk, optb = att_pend.pop(0)
                    emit_pv(oj, ohh, osq, oblk, optb)
                    if oblk == SKB - 1:
                        emit_evac(oj, ohh, osq)

        if not (s == 0 and S * 2 >= 4 * DM):
            for _ in gen_projembed(0):
                pass
        for j in range(NP):
            ga = gen_attention(j)
            gp = gen_projembed(j + 1) if j + 1 < NP else iter(())
            drive((ga, 1), (gp, 1))
            if dbg and j == 0 and s == 0:
                for hh in range(2):
                    dma("sync", dbg_qh[hh], T[f"qh0_{hh}"][:], reads=[f"qh0_{hh}"], writes=["dbgqh"])
                    dma("sync", dbg_ke[hh], T[f"ke0_{hh}"][:], reads=[f"ke0_{hh}"], writes=["dbgke"])
                dma("sync", dbg_vp, T["vp0"][:], reads=["vp0"], writes=["dbgvp"])

        def gen_norm():
            E("vector", "reciprocal", rowsums[:, :], rowsums[:, :],
              reads=["rowsums"], writes=["rowsums"])
            for h in range(NH):
                j, hh = h // 2, h % 2
                for sq in range(NSQ):
                    pp = (h * NSQ + sq) % 2
                    lo = hh * 64
                    if pp == 0:
                        rc_ap, rck = rcps[0:1, 0:SQC], "rcps"
                    else:
                        rc_ap, rck = rcpsB[0:1, 0:SQC], "rcpsB"
                    dma("sync", rc_ap, rowsums[h:h + 1, sq * SQC:(sq + 1) * SQC],
                        reads=["rowsums"], writes=[rck, "pt0", "pt1"] if pp else [rck])
                    rrd = [rck, "pt0", "pt1", "consts"] if pp else [rck, "consts"]
                    for m in range(SQC // MMN):
                        E("tensor", "matmul", ps_s[pp][lo:lo + 64, m * MMN:(m + 1) * MMN],
                          onesf[0:1, 0:64], rc_ap[0:1, m * MMN:(m + 1) * MMN],
                          start=True, stop=True, reads=rrd, writes=[KS[pp]])
                    E("vector", "tensor_mul", attn_in[j][lo:lo + 64, sq * SQC:(sq + 1) * SQC],
                      attn_in[j][lo:lo + 64, sq * SQC:(sq + 1) * SQC], ps_s[pp][lo:lo + 64, 0:SQC],
                      reads=[f"attnin{j}", KS[pp]], writes=[f"attnin{j}"])
                    yield

        if dbg and s == 0:
            for c in range(CK):
                dma("sync", dbg_xn[c], xn_fm[c][:], reads=xnk(0, S), writes=["dbgxn"])
                dma("sync", dbg_attn[c], attn_in[c][:], reads=[f"attnin{c}"], writes=["dbgattn"])
            dma("sync", dbg_rows, rowsums[:], reads=["rowsums"], writes=["dbgrows"])

        def gen_gc():
            for k in range(CK):
                xg = xt[k % 2]
                dma("sync", xg[:, 0:DM // 2], g1_d[k * 128:(k + 1) * 128, :],
                    reads=["dg1"], writes=[f"xt{k % 2}"])
                E("tensor", "matmul", ps_w[0][0:1, 0:DM // 2], gcv[:, k:k + 1],
                  xg[:, 0:DM // 2], start=(k == 0), stop=(k == CK - 1),
                  reads=[f"xt{k % 2}", "gcv"], writes=[KW[0]])
                yield
            E("vector", "tensor_copy", xt[1][0:1, 0:DM // 2], ps_w[0][0:1, 0:DM // 2],
              reads=[KW[0]], writes=["xt1"])
            for ob in range(4):
                E("tensor", "matmul", ps_w[1][:, ob:ob + 1],
                  xt[1][0:1, ob * 128:(ob + 1) * 128], onesf[0:1, 0:1],
                  start=True, stop=True, reads=["xt1", "consts"], writes=[KW[1]])
            for ob in range(4):
                E("scalar", "activation", hg1[:, ob:ob + 1], ps_w[1][:, ob:ob + 1], AF.Gelu,
                  bias=gbf1[:, ob:ob + 1], scale=1.0 / S, reads=[KW[1], "consts"], writes=["hg1"])
            yield
            for ob in range(2):
                for k in range(4):
                    E("tensor", "matmul", ps_w[1][:, ob:ob + 1], gw2t[k][:, ob * 128:(ob + 1) * 128],
                      hg1[:, k:k + 1], start=(k == 0), stop=(k == 3),
                      reads=[f"gw2t{k}", "hg1"], writes=[KW[1]])
            for ob in range(2):
                E("scalar", "activation", hg2[:, ob:ob + 1], ps_w[1][:, ob:ob + 1], AF.Gelu,
                  bias=gbf2[:, ob:ob + 1], scale=1.0, reads=[KW[1], "consts"], writes=["hg2"])
            yield
            for k in range(2):
                E("tensor", "matmul", ps_w[0][0:1, 4:5], gw3t[k][:, 0:1], hg2[:, k:k + 1],
                  start=(k == 0), stop=(k == 1), reads=[f"gw3t{k}", "hg2"], writes=[KW[0]])
            E("scalar", "activation", smallt[0:1, 3:4], ps_w[0][0:1, 4:5], AF.Tanh,
              bias=smallt[0:1, 1:2], scale=0.5, reads=[KW[0], "smallt"], writes=["smallt2"])
            yield
            E("vector", "tensor_mul", smallt[0:1, 4:5], smallt[0:1, 2:3], smallt[0:1, 3:4],
              reads=["smallt", "smallt2"], writes=["smallt3"])
            E("vector", "tensor_scalar", smallt[0:1, 4:5], smallt[0:1, 4:5], 0.7, 0.1, ALU.mult, ALU.add,
              reads=["smallt3"], writes=["smallt3"])
            E("vector", "tensor_scalar", smallt[0:1, 5:6], smallt[0:1, 2:3], 0.3, None, ALU.mult,
              reads=["smallt"], writes=["smallt3"])
            yield

        drive((gen_norm(), 6), (gen_gc(), 1))

        if s == 0:
            for c in range(CK):
                dma("sync", woTt[c][:], wT_s["wo"][c],
                    reads=["woTs0", "woTs1"], writes=[f"woTtp{c}"])

        def gen_wo(q):
            for tq in range(QTB):
                tb = q * QTB + tq
                pp = tb % 2
                xb = f"xt{tb % 2}"
                xb_t = xt[tb % 2]
                dma("sync", xb_t[:], x_d[s, tb * 128:(tb + 1) * 128, :], reads=["dx"], writes=[xb])
                for half in range(DM // 512):
                    for c in range(CK):
                        E("tensor", "matmul", ps_s[pp][:, half * 512:(half + 1) * 512],
                          attn_in[c][:, tb * 128:(tb + 1) * 128],
                          woTt[c][:, half * 512:(half + 1) * 512],
                          start=(c == 0), stop=(c == CK - 1),
                          reads=[f"attnin{c}", f"woTtp{c}"], writes=[KS[pp]])
                    yield
                E("vector", "tensor_add", xb_t[:], ps_s[pp][:, 0:DM], xb_t[:],
                  reads=[KS[pp], xb], writes=[xb])
                dma("sync", ffn_s[s, tb * 128:(tb + 1) * 128, :], xb_t[:],
                    reads=[xb], writes=[f"partD{tb % 2}"], no_waw=True)
                yield

        P.alias_barrier(VIEW_PROJ, VIEW_FFN)
        for k in range(CK):
            cast_load(T[f"w1t_{k}"][:], f"w1t_{k}", w1_d[k * 128:(k + 1) * 128, :], "dw1", DM, k % 2)
            cast_load(T[f"w2t_{k}"][:, 0:DM // 2], f"w2t_{k}", w2_d[k * 128:(k + 1) * 128, :],
                      "dw2", DM // 2, k % 2)

        def gen_ffn(q):
            qoff = q * QT
            for tq in range(QTB):
                tb = q * QTB + tq
                xb = f"xt{tq % 2}"
                xb_t = xt[tq % 2]
                dma("sync", xb_t[:], x_d[s, tb * 128:(tb + 1) * 128, :], reads=["dx"], writes=[xb])
                E("vector", "tensor_scalar", T["stream"][:, tq * DM:(tq + 1) * DM], xb_t[:],
                  rstat[:, tb:tb + 1], mr[:, tb:tb + 1], ALU.mult, ALU.add,
                  reads=[xb, "rstat", "mr"], writes=[f"stream{tq}"])
                yield
            for it in range(3):
                if it == 0:
                    def mv1(k):
                        return xn_fm[k][:, qoff:qoff + QT], xnk(qoff, QT)[0:1][0]
                    def mv1r(k):
                        return xnk(qoff, QT)
                else:
                    for tq in range(QTB):
                        sb = f"sbf{tq % 2}"
                        E("vector", "tensor_copy", T[sb][:], T["stream"][:, tq * DM:(tq + 1) * DM],
                          reads=[f"stream{tq}"], writes=[sb])
                        for c2 in range(CK // 4):
                            pw = (tq * 2 + c2) % 2
                            for cc in range(4):
                                c = c2 * 4 + cc
                                E("tensor", "matmul", ps_w[pw][:, cc * 128:(cc + 1) * 128],
                                  T[sb][:, c * 128:(c + 1) * 128], ident[:], start=True, stop=True,
                                  reads=[sb, "consts"], writes=[KW[pw]])
                            eng2 = "scalar" if pw == 0 else "vector"
                            meth2 = "copy" if pw == 0 else "tensor_copy"
                            for cc in range(4):
                                c = c2 * 4 + cc
                                E(eng2, meth2, T[f"ofm{c}"][:, tq * 128:(tq + 1) * 128],
                                  ps_w[pw][:, cc * 128:(cc + 1) * 128],
                                  reads=[KW[pw]], writes=[f"ofm{c}"])

                    def mv1(k):
                        return T[f"ofm{k}"][:, 0:QT], f"ofm{k}"
                    def mv1r(k):
                        return [f"ofm{k}"]
                for ob in range(CK):
                    pw = ob % 2
                    for k in range(CK):
                        src, _ = mv1(k)
                        E("tensor", "matmul", ps_w[pw][:, 0:QT],
                          T[f"w1t_{k}"][:, ob * 128:(ob + 1) * 128],
                          src, start=(k == 0), stop=(k == CK - 1),
                          reads=[f"w1t_{k}"] + mv1r(k), writes=[KW[pw]])
                    E("scalar", "activation", T[f"h1_{ob}"][:, 0:QT], ps_w[pw][:, 0:QT], AF.Gelu,
                      bias=b1f[:, ob:ob + 1], scale=1.0, reads=[KW[pw], "consts"], writes=[f"h1_{ob}"])
                    yield
                for ob in range(CK // 2):
                    pw = ob % 2
                    for k in range(CK):
                        E("tensor", "matmul", ps_w[pw][:, 0:QT],
                          T[f"w2t_{k}"][:, ob * 128:(ob + 1) * 128],
                          T[f"h1_{k}"][:, 0:QT], start=(k == 0), stop=(k == CK - 1),
                          reads=[f"w2t_{k}", f"h1_{k}"], writes=[KW[pw]])
                    E("scalar", "activation", T[f"h2_{ob}"][:, 0:QT], ps_w[pw][:, 0:QT], AF.Gelu,
                      bias=b2f[:, ob:ob + 1], scale=1.0, reads=[KW[pw], "consts"], writes=[f"h2_{ob}"])
                    yield
                for ob in range(CK // 4):
                    pw = ob % 2
                    for k in range(CK // 2):
                        E("tensor", "matmul", ps_w[pw][:, 0:QT],
                          w3t[k][:, ob * 128:(ob + 1) * 128],
                          T[f"h2_{k}"][:, 0:QT], start=(k == 0), stop=(k == CK // 2 - 1),
                          reads=[f"w3t{k}", f"h2_{k}"], writes=[KW[pw]])
                    E("scalar", "activation", T[f"h3_{ob}"][:, 0:QT], ps_w[pw][:, 0:QT], AF.Gelu,
                      bias=b3f[:, ob:ob + 1], scale=1.0, reads=[KW[pw], "consts"], writes=[f"h3_{ob}"])
                    yield
                for k in range(CK // 4):
                    E("tensor", "matmul", ps_w[0][0:1, 0:QT], w4t[k][:, 0:1], T[f"h3_{k}"][:, 0:QT],
                      start=(k == 0), stop=(k == CK // 4 - 1),
                      reads=[f"w4t{k}", f"h3_{k}"], writes=[KW[0]])
                E("scalar", "activation", T["cpt"][0:1, 0:QT], ps_w[0][0:1, 0:QT], AF.Tanh,
                  bias=smallt[0:1, 0:1], scale=0.5, reads=[KW[0], "smallt"], writes=["cpt"])
                E("vector", "tensor_scalar", T["cpt"][0:1, 0:QT], T["cpt"][0:1, 0:QT],
                  smallt[0:1, 5:6], smallt[0:1, 4:5], ALU.mult, ALU.add,
                  reads=["cpt", "smallt3"], writes=["cpt"])
                for tq in range(QTB):
                    E("tensor", "matmul", ps_w[1][:, tq:tq + 1],
                      T["cpt"][0:1, tq * 128:(tq + 1) * 128], onesf[0:1, 0:1],
                      start=True, stop=True, reads=["cpt", "consts"], writes=[KW[1]])
                E("vector", "tensor_copy", T["cptm"][:, 0:QTB], ps_w[1][:, 0:QTB],
                  reads=[KW[1]], writes=["cptm"])
                E("vector", "tensor_scalar", T["ncptm"][:, 0:QTB], T["cptm"][:, 0:QTB], -1.0, None,
                  ALU.mult, reads=["cptm"], writes=["ncptm"])
                for tq in range(QTB):
                    st = T["stream"]
                    o = tq * DM
                    sk2 = f"stream{tq}"
                    t01k = f"t01_{tq % 2}"
                    t01s = T["t01"][:, (tq % 2) * 256:(tq % 2) * 256 + 256]
                    t01s2 = T["t01"][:, (tq % 2) * 256 + 0:(tq % 2) * 256 + 256]
                    cp = T["cptm"][:, tq:tq + 1]
                    ncp = T["ncptm"][:, tq:tq + 1]
                    E("vector", "tensor_copy", T["t01"][:, 0:512], st[:, o:o + 512],
                      reads=[sk2], writes=["t01"]) if False else None
                    E("vector", "scalar_tensor_tensor", st[:, o + 512:o + 768], st[:, o:o + 256],
                      ncp, st[:, o + 512:o + 768], ALU.mult, ALU.add,
                      reads=[sk2, "ncptm"], writes=[sk2]) if False else None
                    E("vector", "tensor_copy", T["t01"][:, 0:512], st[:, o:o + 512],
                      reads=[sk2], writes=["t01"])
                    E("vector", "scalar_tensor_tensor", st[:, o:o + 256], st[:, o + 512:o + 768],
                      cp, st[:, o:o + 256], ALU.mult, ALU.add,
                      reads=[sk2, "cptm"], writes=[sk2])
                    E("vector", "scalar_tensor_tensor", st[:, o + 256:o + 512],
                      st[:, o + 768:o + 1024], ncp, st[:, o + 256:o + 512], ALU.mult, ALU.add,
                      reads=[sk2, "ncptm"], writes=[sk2])
                    E("vector", "scalar_tensor_tensor", st[:, o + 512:o + 768], T["t01"][:, 0:256],
                      ncp, st[:, o + 512:o + 768], ALU.mult, ALU.add,
                      reads=[sk2, "t01", "ncptm"], writes=[sk2])
                    E("vector", "scalar_tensor_tensor", st[:, o + 768:o + 1024],
                      T["t01"][:, 256:512], cp, st[:, o + 768:o + 1024], ALU.mult, ALU.add,
                      reads=[sk2, "t01", "cptm"], writes=[sk2])
                    yield
        def gen_final(q):
            for tq in range(QTB):
                tb = q * QTB + tq
                xb = f"xt{tq % 2}"
                xb_t = xt[tq % 2]
                dma("sync", xb_t[:], ffn_s[s, tb * 128:(tb + 1) * 128, :],
                    reads=["partD0", "partD1"], writes=[xb])
                E("vector", "tensor_add", T["stream"][:, tq * DM:(tq + 1) * DM],
                  T["stream"][:, tq * DM:(tq + 1) * DM], xb_t[:],
                  reads=[f"stream{tq}", xb], writes=[f"stream{tq}"])
                dma("sync", out_d[s, tb * 128:(tb + 1) * 128, :],
                    T["stream"][:, tq * DM:(tq + 1) * DM],
                    reads=[f"stream{tq}"], writes=[f"outD{tq}"], no_waw=True)
                yield

        for q in range(NQ):
            drive((gen_ffn(q), 4), (gen_wo(q), 1))
            for _ in gen_final(q):
                pass

    P.wait_only("sync", reads=[f"outD{i}" for i in range(QTB)] + (["dbgxn", "dbgattn", "dbgrows",
                                          "dbgqh", "dbgke", "dbgvp"] if dbg else []))

    P.finalize()
    with nc.Block() as block:
        P.replay(block)
    return nc


_NC_CACHE = {}


def _get_nc(S, BATCH):
    key = (S, BATCH)
    if key not in _NC_CACHE:
        _NC_CACHE[key] = build_program(S=S, BATCH=BATCH)
    return _NC_CACHE[key]


def kernel(L0_attn_x, L0_attn_wq, L0_attn_wk, L0_attn_wv, L0_attn_wo, L0_attn_pope_bias,
           L0_ffn_cn_w1, L0_ffn_cn_b1, L0_ffn_cn_w2, L0_ffn_cn_b2, L0_ffn_cn_w3, L0_ffn_cn_b3,
           L0_ffn_cn_w4, L0_ffn_cn_b4, L0_ffn_gc_w1, L0_ffn_gc_b1, L0_ffn_gc_w2, L0_ffn_gc_b2,
           L0_ffn_gc_w3, L0_ffn_gc_b3, L0_ffn_adaptive_range):
    x = np.ascontiguousarray(np.asarray(L0_attn_x, dtype=np.float32))
    B_total, S, DM = x.shape
    n_cores = 8
    bpc = B_total // n_cores
    nc = _get_nc(S, bpc)

    def f32(a):
        return np.ascontiguousarray(np.asarray(a, np.float32))

    shared = {
        "wq": f32(L0_attn_wq), "wk": f32(L0_attn_wk), "wv": f32(L0_attn_wv),
        "wo": f32(L0_attn_wo), "pope": f32(L0_attn_pope_bias),
        "cnw1": f32(L0_ffn_cn_w1), "cnb1": f32(L0_ffn_cn_b1),
        "cnw2": f32(L0_ffn_cn_w2), "cnb2": f32(L0_ffn_cn_b2),
        "cnw3": f32(L0_ffn_cn_w3), "cnb3": f32(L0_ffn_cn_b3),
        "cnw4": f32(L0_ffn_cn_w4), "cnb4": f32(L0_ffn_cn_b4),
        "gcw1": f32(L0_ffn_gc_w1), "gcb1": f32(L0_ffn_gc_b1),
        "gcw2": f32(L0_ffn_gc_w2), "gcb2": f32(L0_ffn_gc_b2),
        "gcw3": f32(L0_ffn_gc_w3), "gcb3": f32(L0_ffn_gc_b3),
        "ar": f32(L0_ffn_adaptive_range),
    }
    in_maps = []
    for c in range(n_cores):
        im = dict(shared)
        im["x"] = np.ascontiguousarray(x[c * bpc:(c + 1) * bpc])
        in_maps.append(im)
    res = run_bass_kernel_spmd(nc, in_maps, core_ids=list(range(n_cores)))
    return np.concatenate([r["out"] for r in res.results], axis=0)


# version 35
# speedup vs baseline: 1.0182x; 1.0061x over previous
import numpy as np
import ml_dtypes

import concourse.bass as bass
import concourse.mybir as mybir
from concourse.bass_utils import run_bass_kernel_spmd

F32 = mybir.dt.float32
BF16 = mybir.dt.bfloat16
AF = mybir.ActivationFunctionType
ALU = mybir.AluOpType
AX = mybir.AxisListType
PI = float(np.pi)

ENGINES = ("tensor", "scalar", "vector", "gpsimd", "sync")


class _Op:
    __slots__ = ("method", "args", "kwargs", "deps", "inc_pos", "waits", "note")

    def __init__(self, method, args, kwargs, note=""):
        self.method = method
        self.args = args
        self.kwargs = kwargs
        self.deps = []
        self.inc_pos = None
        self.waits = []
        self.note = note


class Prog:

    def __init__(self, nc):
        self.nc = nc
        self.ops = {e: [] for e in ENGINES}
        self.lastw = {}
        self.readers = {}
        self.pending = {}
        self.dma_slots = []
        self.dma_uses = []
        self.slot_for_key = {}
        self.eng_sems = {e: nc.alloc_semaphore(f"es_{e}") for e in ENGINES}
        self.slot_cons = {}

    def _slot_for(self, key):
        s = self.slot_for_key.get(key)
        if s is None:
            s = len(self.dma_slots)
            self.dma_slots.append(self.nc.alloc_semaphore(f"dsl_{key}"))
            self.dma_uses.append(0)
            self.slot_for_key[key] = s
        return s

    @staticmethod
    def _dep(deps, ref):
        if ref is not None and ref not in deps:
            deps.append(ref)

    def emit(self, eng, method, *args, reads=(), writes=(), dma=False, no_waw=False,
             note="", **kwargs):
        op = _Op(method, args, kwargs, note)
        deps = op.deps
        for k in reads:
            if k in self.pending:
                for r in self.pending.pop(k):
                    self._dep(deps, r)
            self._dep(deps, self.lastw.get(k))
        for k in writes:
            if k in self.pending:
                for r in self.pending.pop(k):
                    self._dep(deps, r)
            if not no_waw:
                self._dep(deps, self.lastw.get(k))
            for r in self.readers.get(k, ()):
                self._dep(deps, r)
        if eng == "tensor":
            op.deps = deps = [d for d in deps if not (d[0] == "E" and d[1] == "tensor")]
        data_deps = list(deps)
        if dma:
            slot = self._slot_for(writes[0])
            for r in self.slot_cons.get(slot, ()):
                self._dep(deps, r)
            self.slot_cons[slot] = []
            self.dma_uses[slot] += 1
            ref = ("D", slot, self.dma_uses[slot])
            op.inc_pos = ("D", slot)
        else:
            ref = ("E", eng, len(self.ops[eng]))
        self.ops[eng].append(op)
        for d in data_deps:
            if d[0] == "D":
                self.slot_cons.setdefault(d[1], []).append(ref)
        for k in reads:
            self.readers.setdefault(k, []).append(ref)
        for k in writes:
            self.lastw[k] = ref
            if not no_waw:
                self.readers[k] = []
        return ref

    def wait_only(self, eng, reads=(), note=""):
        self.emit(eng, None, reads=reads, note=note)

    def alias_barrier(self, old_keys, new_keys):
        refs = []
        for k in old_keys:
            w = self.lastw.get(k)
            if w is not None and w not in refs:
                refs.append(w)
            for r in self.readers.get(k, ()):
                if r not in refs:
                    refs.append(r)
        for k in new_keys:
            cur = self.pending.setdefault(k, [])
            for r in refs:
                if r not in cur:
                    cur.append(r)

    def finalize(self):
        needed = {e: set() for e in ENGINES}
        for eng in ENGINES:
            frontier = {}
            for op in self.ops[eng]:
                dmax = {}
                for ref in op.deps:
                    key = (ref[0], ref[1])
                    if dmax.get(key, -1) < ref[2]:
                        dmax[key] = ref[2]
                w = []
                for key, n in dmax.items():
                    ref = (key[0], key[1], n)
                    if frontier.get(key, -1) >= n:
                        continue
                    frontier[key] = n
                    w.append(ref)
                for ref in w:
                    if ref[0] == "E":
                        needed[ref[1]].add(ref[2])
                op.waits = w
        pos = {}
        for eng in ENGINES:
            cnt = 0
            marks = needed[eng]
            for i, op in enumerate(self.ops[eng]):
                if i in marks:
                    if op.method is None:
                        raise RuntimeError(f"pure-wait op on {eng} is a dep target")
                    cnt += 1
                    if op.inc_pos is None:
                        op.inc_pos = ("E", eng)
                    pos[(eng, i)] = cnt
        for eng in ENGINES:
            for op in self.ops[eng]:
                tw = []
                for ref in op.waits:
                    if ref[0] == "E":
                        tw.append((self.eng_sems[ref[1]], pos[(ref[1], ref[2])]))
                    else:
                        tw.append((self.dma_slots[ref[1]], 16 * ref[2]))
                op.waits = tw

    def replay(self, block):
        def runner(eng_name):
            def _run(engine):
                for op in self.ops[eng_name]:
                    for (sem, val) in op.waits:
                        engine.wait_ge(sem, val)
                    if op.method is None:
                        continue
                    inst = getattr(engine, op.method)(*op.args, **op.kwargs)
                    if op.inc_pos is not None:
                        kind, tgt = op.inc_pos
                        if kind == "D":
                            inst.then_inc(self.dma_slots[tgt], 16)
                        else:
                            inst.then_inc(self.eng_sems[tgt], 1)
            return _run

        block.tensor(runner("tensor"))
        block.scalar(runner("scalar"))
        block.vector(runner("vector"))
        block.gpsimd(runner("gpsimd"))
        block.sync(runner("sync"))


def build_program(S=2048, BATCH=2, DM=1024, NH=16, dbg=False):
    D = DM // NH
    CK = DM // 128
    NP = NH // 2
    TB = S // 128
    SKB = S // 128
    SQC = min(1024, S)
    NSQ = S // SQC
    NQ = 4 if TB >= 4 else TB
    QTB = TB // NQ
    QT = QTB * 128
    MMN = min(512, S)
    NCH = S // MMN
    EXP_MARGIN = 2.0

    nc = bass.Bass("TRN2", target_bir_lowering=False)

    x_d = nc.dram_tensor("x", [BATCH, S, DM], F32, kind="ExternalInput").ap()
    wq_d = nc.dram_tensor("wq", [DM, DM], F32, kind="ExternalInput").ap()
    wk_d = nc.dram_tensor("wk", [DM, DM], F32, kind="ExternalInput").ap()
    wv_d = nc.dram_tensor("wv", [DM, DM], F32, kind="ExternalInput").ap()
    wo_d = nc.dram_tensor("wo", [DM, DM], F32, kind="ExternalInput").ap()
    pope_d = nc.dram_tensor("pope", [NH, D], F32, kind="ExternalInput").ap()
    w1_d = nc.dram_tensor("cnw1", [DM, DM], F32, kind="ExternalInput").ap()
    b1_d = nc.dram_tensor("cnb1", [DM], F32, kind="ExternalInput").ap()
    w2_d = nc.dram_tensor("cnw2", [DM, DM // 2], F32, kind="ExternalInput").ap()
    b2_d = nc.dram_tensor("cnb2", [DM // 2], F32, kind="ExternalInput").ap()
    w3_d = nc.dram_tensor("cnw3", [DM // 2, DM // 4], F32, kind="ExternalInput").ap()
    b3_d = nc.dram_tensor("cnb3", [DM // 4], F32, kind="ExternalInput").ap()
    w4_d = nc.dram_tensor("cnw4", [DM // 4, 1], F32, kind="ExternalInput").ap()
    b4_d = nc.dram_tensor("cnb4", [1], F32, kind="ExternalInput").ap()
    g1_d = nc.dram_tensor("gcw1", [DM, DM // 2], F32, kind="ExternalInput").ap()
    gb1_d = nc.dram_tensor("gcb1", [DM // 2], F32, kind="ExternalInput").ap()
    g2_d = nc.dram_tensor("gcw2", [DM // 2, DM // 4], F32, kind="ExternalInput").ap()
    gb2_d = nc.dram_tensor("gcb2", [DM // 4], F32, kind="ExternalInput").ap()
    g3_d = nc.dram_tensor("gcw3", [DM // 4, 1], F32, kind="ExternalInput").ap()
    gb3_d = nc.dram_tensor("gcb3", [1], F32, kind="ExternalInput").ap()
    ar_d = nc.dram_tensor("ar", [1], F32, kind="ExternalInput").ap()
    out_d = nc.dram_tensor("out", [BATCH, S, DM], F32, kind="ExternalOutput").ap()
    wT_s = {w: nc.dram_tensor(f"{w}T_s", [CK, 128, DM], BF16, kind="Internal").ap()
            for w in ("wq", "wk", "wv", "wo")}
    ffn_s = nc.dram_tensor("ffn_s", [BATCH, S, DM], F32, kind="Internal").ap()
    if dbg:
        dbg_xn = nc.dram_tensor("dbg_xn", [CK, 128, S], BF16, kind="ExternalOutput").ap()
        dbg_qh = nc.dram_tensor("dbg_qh", [2, 128, S], BF16, kind="ExternalOutput").ap()
        dbg_ke = nc.dram_tensor("dbg_ke", [2, 128, S], BF16, kind="ExternalOutput").ap()
        dbg_vp = nc.dram_tensor("dbg_vp", [128, (S // 128) * 130], BF16, kind="ExternalOutput").ap()
        dbg_rows = nc.dram_tensor("dbg_rows", [16, S], F32, kind="ExternalOutput").ap()
        dbg_expb = nc.dram_tensor("dbg_expb", [128, NH], F32, kind="ExternalOutput").ap()
        dbg_attn = nc.dram_tensor("dbg_attn", [CK, 128, S], BF16, kind="ExternalOutput").ap()

    inv = 10000.0 ** (-(np.arange(D, dtype=np.float64) / D))
    fr = np.arange(S, dtype=np.float64)[None, :] * inv[:, None]
    t1_np = np.concatenate([np.cos(fr), np.sin(fr)], 0).astype(ml_dtypes.bfloat16)
    t1s_np = np.concatenate([np.sin(fr), np.cos(fr)], 0).astype(ml_dtypes.bfloat16)
    T1_d = nc.inline_tensor(t1_np, "T1c").ap()
    T1s_d = nc.inline_tensor(t1s_np, "T1sc").ap()
    ident_d = nc.inline_tensor(np.eye(128, dtype=ml_dtypes.bfloat16), "identc").ap()
    i16_d = nc.inline_tensor(np.eye(16, dtype=np.float32), "i16c").ap()
    onesf_d = nc.inline_tensor(np.ones((1, 128), dtype=np.float32), "onesfc").ap()
    o2 = np.zeros((128, 2), dtype=ml_dtypes.bfloat16)
    o2[0:64, 0] = 1.0
    o2[64:128, 1] = 1.0
    ones2_d = nc.inline_tensor(o2, "ones2c").ap()

    A = nc.alloc_sbuf_tensor
    cpool = A("cpool", [128, 5], F32)
    T1 = A("T1", [128, S], BF16)
    T1s = A("T1s", [128, S], BF16)
    ident = A("ident", [128, 128], BF16)
    i16 = A("i16", [16, 16], F32)
    onesf = A("onesf", [1, 128], F32)
    ones2 = A("ones2", [128, 2], BF16)
    pbt = A("pbt", [16, D], F32)
    bcl = A("bcl", [16, D], F32)
    btr = A("btr", [D, 16], F32)
    sbv = A("sbv", [D, 16], F32)
    cbv = A("cbv", [D, 16], F32)
    ttv = A("ttv", [D, 16], F32)
    cbdup = A("cbdup", [128, NH], F32)
    sbdup = A("sbdup", [128, NH], F32)
    mstat = A("mstat", [128, TB], F32)
    rstat = A("rstat", [128, TB], F32)
    mr = A("mr", [128, TB], F32)
    rowsums = A("rowsums", [16, S], F32)
    rcps = A("rcps", [1, SQC], F32)
    gcv = A("gcv", [128, CK], F32)
    gcvb = A("gcvb", [128, CK], BF16)
    b1f = A("b1f", [128, CK], F32)
    b2f = A("b2f", [128, CK // 2], F32)
    b3f = A("b3f", [128, CK // 4], F32)
    gbf1 = A("gbf1", [128, CK // 2], F32)
    gbf2 = A("gbf2", [128, CK // 4], F32)
    smallt = A("smallt", [1, 8], F32)
    tmpB = [A(f"tmpB{i}", [64, SQC], BF16) for i in range(2)]
    rowstg0 = A("rowstg0", [65, SQC], F32)
    woTt = [A(f"woTtp{c}", [128, DM], BF16) for c in range(CK)]
    w3t = [A(f"w3t{k}", [128, DM // 4], BF16) for k in range(4)]
    w4t = [A(f"w4t{k}", [128, 1], BF16) for k in range(2)]
    gw2t = [A(f"gw2t{k}", [128, DM // 4], BF16) for k in range(4)]
    gw3t = [A(f"gw3t{k}", [128, 1], BF16) for k in range(2)]
    hg1 = A("hg1", [128, 4], BF16)
    hg2 = A("hg2", [128, 2], BF16)
    xn_fm = [A(f"xnfm{c}", [128, S], BF16) for c in range(CK)]
    attn_in = [A(f"attnin{c}", [128, S], BF16) for c in range(CK)]
    xt = [A(f"xt{i}", [128, DM], F32) for i in range(2)]
    xsq = A("xsq", [128, DM], F32)
    v1t = [A(f"v1t{i}", [128, 1], F32) for i in range(2)]

    def rnd(x):
        return (x + 63) // 64 * 64

    lay = {}

    def place(name, shape, dtype, cursor):
        nbytes = shape[1] * (2 if dtype == BF16 else 4)
        lay[name] = (shape, dtype, cursor)
        return cursor + rnd(nbytes)

    arena_sz = 0
    cur = 0
    cur = place("qmag", [128, S], BF16, cur)
    cur = place("kmag", [128, S], BF16, cur)
    cur = place("tmp1", [128, S], BF16, cur)
    cur = place("tmpe", [128, max(MMN, DM)], F32, cur)
    for b in range(2):
        for h in range(2):
            cur = place(f"qh{b}_{h}", [128, S], BF16, cur)
            cur = place(f"ke{b}_{h}", [128, S], BF16, cur)
    for b in range(2):
        cur = place(f"vp{b}", [128, SKB * 130], BF16, cur)
    for b in range(3):
        cur = place(f"pt{b}", [128, SQC], BF16, cur)
    for b in range(2):
        for w in ("wq", "wk", "wv"):
            cur = place(f"{w}t{b}", [128, DM], BF16, cur)
    cur = place("tmpv0", [128, MMN], BF16, cur)
    cur = place("tmpv1", [128, MMN], BF16, cur)
    cur = place("xnb0", [128, DM], BF16, cur)
    cur = place("xnb1", [128, DM], BF16, cur)
    arena_sz = max(arena_sz, cur)
    VIEW_PROJ = ["qmag", "kmag", "tmp1", "tmpe", "tmpv0", "tmpv1",
                 "xnb0", "xnb1", "vp0", "vp1", "pt0", "pt1", "pt2"] \
        + [f"qh{b}_{h}" for b in range(2) for h in range(2)] \
        + [f"ke{b}_{h}" for b in range(2) for h in range(2)] \
        + [f"{w}t{b}" for b in range(2) for w in ("wq", "wk", "wv")]

    cur = 0
    cur = place("stream", [128, QTB * DM], F32, cur)
    for c in range(CK):
        cur = place(f"ofm{c}", [128, QT], BF16, cur)
    cur = place("sbf0", [128, DM], BF16, cur)
    cur = place("sbf1", [128, DM], BF16, cur)
    for c in range(CK):
        cur = place(f"h1_{c}", [128, QT], BF16, cur)
    for c in range(CK // 2):
        cur = place(f"h2_{c}", [128, QT], BF16, cur)
    for c in range(CK // 4):
        cur = place(f"h3_{c}", [128, QT], BF16, cur)
    cur = place("t01", [128, 512], F32, cur)
    cur = place("cpt", [1, QT], F32, cur)
    cur = place("cptm", [128, QTB], F32, cur)
    cur = place("ncptm", [128, QTB], F32, cur)
    for k in range(CK):
        cur = place(f"w1t_{k}", [128, DM], BF16, cur)
        cur = place(f"w2t_{k}", [128, DM // 2], BF16, cur)
        cur = place(f"gw1t_{k}", [128, DM // 2], BF16, cur)
    arena_sz = max(arena_sz, cur)
    VIEW_FFN = ([f"stream{i}" for i in range(QTB)]
                + ["sbf0", "sbf1", "t01", "cpt", "cptm", "ncptm"]
                + [f"ofm{c}" for c in range(CK)]
                + [f"h1_{c}" for c in range(CK)]
                + [f"h2_{c}" for c in range(CK // 2)]
                + [f"h3_{c}" for c in range(CK // 4)]
                + [f"w1t_{k}" for k in range(CK)]
                + [f"w2t_{k}" for k in range(CK)]
                + [f"gw1t_{k}" for k in range(CK)])

    VIEW_WO = []

    base, _ = nc.bump_sbuf(arena_sz)
    T = {}
    for name, (shape, dtype, off) in lay.items():
        T[name] = nc.alloc_sbuf_tensor_at(name, shape, dtype, offset=base + off)
    rcpsB = nc.alloc_sbuf_tensor_at("rcpsB", [1, SQC], F32, offset=base + lay["pt0"][2])

    ps_s = [nc.alloc_psum_tensor("ps_s0", [128, 1024], F32),
            nc.alloc_psum_tensor("ps_s1", [128, 1024], F32)]
    ps_pv = nc.alloc_psum_tensor("ps_pv", [65, 1024], F32)
    ps_w = [nc.alloc_psum_tensor("ps_w0", [128, 512], F32),
            nc.alloc_psum_tensor("ps_w1", [128, 512], F32)]
    KW = ["ps_w0", "ps_w1"]
    KS = ["ps_s0", "ps_s1"]

    P = Prog(nc)
    E = P.emit

    def dma(queue, dst, src, reads, writes, note="", no_waw=False, **kw):
        return E(queue, "dma_start", dst, src, reads=reads, writes=writes, dma=True,
                 no_waw=no_waw, note=note, **kw)

    E("vector", "memset", cpool[:, 0:1], 1e-5, writes=["cpool"])
    E("vector", "memset", cpool[:, 1:2], 0.5 * PI, reads=["cpool"], writes=["cpool"])
    E("vector", "memset", cpool[:, 2:3], PI, reads=["cpool"], writes=["cpool"])
    E("vector", "memset", cpool[:, 3:4], 1.0, reads=["cpool"], writes=["cpool"])
    E("vector", "memset", cpool[:, 4:5], -8.0, reads=["cpool"], writes=["cpool"])
    dma("sync", T1[:], T1_d, reads=["dT1"], writes=["consts"], no_waw=True)
    dma("sync", T1s[:], T1s_d, reads=["dT1s"], writes=["consts"], no_waw=True)
    dma("sync", ident[:], ident_d, reads=["did"], writes=["consts"], no_waw=True)
    dma("sync", i16[:], i16_d, reads=["di16"], writes=["consts"], no_waw=True)
    dma("sync", onesf[:], onesf_d, reads=["donesf"], writes=["consts"], no_waw=True)
    dma("sync", ones2[:], ones2_d, reads=["dones2"], writes=["consts"], no_waw=True)
    dma("sync", pbt[:], pope_d, reads=["dpope"], writes=["pbt"])
    dma("sync", b1f[:], b1_d.rearrange("(j p) -> p j", p=128), reads=["db1"], writes=["consts"], no_waw=True, allow_slow_non_contiguous=True)
    dma("sync", b2f[:], b2_d.rearrange("(j p) -> p j", p=128), reads=["db2"], writes=["consts"], no_waw=True, allow_slow_non_contiguous=True)
    dma("sync", b3f[:], b3_d.rearrange("(j p) -> p j", p=128), reads=["db3"], writes=["consts"], no_waw=True, allow_slow_non_contiguous=True)
    dma("sync", gbf1[:], gb1_d.rearrange("(j p) -> p j", p=128), reads=["dgb1"], writes=["consts"], no_waw=True, allow_slow_non_contiguous=True)
    dma("sync", gbf2[:], gb2_d.rearrange("(j p) -> p j", p=128), reads=["dgb2"], writes=["consts"], no_waw=True, allow_slow_non_contiguous=True)
    dma("sync", smallt[0:1, 0:1], b4_d[None, :], reads=["db4"], writes=["smallt"])
    dma("sync", smallt[0:1, 1:2], gb3_d[None, :], reads=["dgb3"], writes=["smallt"])
    dma("sync", smallt[0:1, 2:3], ar_d[None, :], reads=["dar"], writes=["smallt"])
    E("vector", "tensor_scalar", smallt[0:1, 0:2], smallt[0:1, 0:2], 0.5, None, ALU.mult,
      reads=["smallt"], writes=["smallt"])
    def cast_load(dst_ap, dst_key, src_ap, src_key, width, stg_i):
        dma("gpsimd", dst_ap, src_ap, reads=[src_key], writes=[dst_key])

    for k in range(4):
        cast_load(w3t[k][:], f"w3t{k}", w3_d[k * 128:(k + 1) * 128, :], "dw3", DM // 4, k % 2)
        cast_load(gw2t[k][:], f"gw2t{k}", g2_d[k * 128:(k + 1) * 128, :], "dg2", DM // 4, k % 2)
    for k in range(2):
        cast_load(w4t[k][:], f"w4t{k}", w4_d[k * 128:(k + 1) * 128, :], "dw4", 1, k % 2)
        cast_load(gw3t[k][:], f"gw3t{k}", g3_d[k * 128:(k + 1) * 128, :], "dg3", 1, k % 2)

    E("vector", "tensor_scalar", bcl[:], pbt[:], 0.0, -2 * PI, ALU.min, ALU.max,
      reads=["pbt"], writes=["bcl"])
    E("tensor", "matmul", ps_w[0][0:D, 0:16], bcl[:], i16[:], start=True, stop=True,
      reads=["bcl", "consts"], writes=[KW[0]])
    E("vector", "tensor_copy", btr[:], ps_w[0][0:D, 0:16], reads=[KW[0]], writes=["btr"])
    E("scalar", "activation", sbv[:], btr[:], AF.Sin, bias=cpool[0:D, 2:3], scale=1.0,
      reads=["btr", "cpool"], writes=["sbv"])
    E("scalar", "activation", ttv[:], btr[:], AF.Abs, bias=cpool[0:D, 2:3], scale=1.0,
      reads=["btr", "cpool"], writes=["ttv"])
    E("scalar", "activation", cbv[:], ttv[:], AF.Sin, bias=cpool[0:D, 1:2], scale=-1.0,
      reads=["ttv", "cpool"], writes=["cbv"])
    E("vector", "tensor_scalar", cbv[:], cbv[:], -1.0, None, ALU.mult, reads=["cbv"], writes=["cbv"])
    E("vector", "tensor_scalar", sbv[:], sbv[:], -1.0, None, ALU.mult, reads=["sbv"], writes=["sbv"])
    dma("sync", cbdup[0:64, :], cbv[:], reads=["cbv"], writes=["cbdup"])
    dma("sync", cbdup[64:128, :], cbv[:], reads=["cbv"], writes=["cbdup"])
    dma("sync", sbdup[0:64, :], sbv[:], reads=["sbv"], writes=["sbdup"])
    E("vector", "tensor_scalar", ttv[:], sbv[:], -1.0, None, ALU.mult, reads=["sbv"], writes=["ttv"])
    dma("sync", sbdup[64:128, :], ttv[:], reads=["ttv"], writes=["sbdup"])

    if S * 2 >= 4 * DM:
        prepCb = nc.alloc_sbuf_tensor_at("prepCb", [128, DM], BF16, offset=base + lay["qh1_0"][2])
        prepDb = nc.alloc_sbuf_tensor_at("prepDb", [128, DM], BF16, offset=base + lay["qh1_1"][2])
    else:
        prepCb = prepDb = None

    def gen_prep(olap):
        names = {"wq": wq_d, "wk": wk_d, "wv": wv_d, "wo": wo_d}
        if olap:
            order = [(w, i) for i in range(CK) for w in ("wq", "wk", "wv")] \
                + [("wo", i) for i in range(CK)]
        else:
            order = [(w, i) for w in ("wq", "wk", "wv", "wo") for i in range(CK)]
        if olap:
            bufs = [(T["wqt1"], "wqt1"), (T["wkt1"], "wkt1"),
                    (prepCb, "qh1_0"), (prepDb, "qh1_1")]
            nb = 4
            stg4 = [(T["ke1_0"][:, 0:DM], ["stgA", "ke1_0"]),
                    (T["ke1_1"][:, 0:DM], ["stgB", "ke1_1"]),
                    (T["ke1_0"][:, DM:2 * DM], ["stgC", "ke1_0"]),
                    (T["ke1_1"][:, DM:2 * DM], ["stgD", "ke1_1"])]
            ns = 4
        elif S >= DM:
            bufs = [(T["qh0_0"], "qh0_0"), (T["qh0_1"], "qh0_1")]
            nb = 2
        else:
            bufs = [(T["xnb0"], "xnb0"), (T["xnb1"], "xnb1")]
            nb = 2
        for t, (wname, i) in enumerate(order):
            wd = names[wname]
            if True:
                wtmp_ap, wtmp = bufs[t % nb]
                if olap:
                    stg_ap, stg_keys = stg4[t % ns]
                else:
                    stg = "wqt0" if t % 2 == 0 else "wqt1"
                    stg_ap, stg_keys = T[stg][:, 0:DM], [stg]
                dma("gpsimd", wtmp_ap[:, 0:DM], wd[i * 128:(i + 1) * 128, :],
                    reads=[f"d{wname}"], writes=[f"prepc{t % nb}", wtmp])
                yield
                for j2 in range(CK // 4):
                    pw = j2 % 2
                    for jj in range(4):
                        j = j2 * 4 + jj
                        E("tensor", "matmul", ps_w[pw][:, jj * 128:(jj + 1) * 128],
                          wtmp_ap[:, j * 128:(j + 1) * 128], ident[:], start=True, stop=True,
                          reads=[wtmp, "consts"], writes=[KW[pw]])
                    E("vector", "tensor_copy", stg_ap[:, j2 * 512:(j2 + 1) * 512], ps_w[pw][:, :],
                      reads=[KW[pw]], writes=stg_keys, no_waw=len(stg_keys) > 1)
                    yield
                if wname == "wo":
                    for j in range(CK):
                        dma("sync", wT_s[wname][j][:, i * 128:(i + 1) * 128],
                            stg_ap[:, j * 128:(j + 1) * 128],
                            reads=stg_keys, writes=[f"{wname}Ts{i % 2}"], no_waw=True)
                else:
                    dma("sync", wT_s[wname][i], stg_ap[:, 0:DM], reads=stg_keys,
                        writes=[f"{wname}Ts{i % 2}"], no_waw=True)
                yield

    def xnk(tok_lo, n_tok):
        g_lo, g_hi = tok_lo // MMN, (tok_lo + n_tok - 1) // MMN
        return [f"xnfm{c}g{g}" for c in range(CK) for g in range(g_lo, g_hi + 1)]

    def drive(*pairs):
        gens = [g for g, n in pairs]
        rates = [n for g, n in pairs]
        done = [False] * len(gens)
        while not all(done):
            for gi in range(len(gens)):
                for _ in range(rates[gi]):
                    if not done[gi]:
                        done[gi] = next(gens[gi], _SENT) is _SENT
    _SENT = object()

    for s in range(BATCH):
        P.alias_barrier(VIEW_FFN, VIEW_PROJ)

        def gen_ln():
            for tb in range(TB):
                xb = f"xt{tb % 2}"
                xb_t = xt[tb % 2]
                dma("sync", xb_t[:], x_d[s, tb * 128:(tb + 1) * 128, :], reads=["dx"], writes=[xb])
                E("vector", "reduce_sum", mstat[:, tb:tb + 1], xb_t[:], axis=AX.X,
                  reads=[xb], writes=["mstat"])
                E("scalar", "activation", xsq[:], xb_t[:], AF.Square, bias=0.0, scale=1.0,
                  reads=[xb], writes=["xsq"])
                E("vector", "reduce_sum", rstat[:, tb:tb + 1], xsq[:], axis=AX.X,
                  reads=["xsq"], writes=["rstat"])
                yield
            E("vector", "tensor_scalar", mstat[:, :], mstat[:, :], 1.0 / DM, None, ALU.mult,
              reads=["mstat"], writes=["mstat"])
            E("vector", "tensor_mul", mr[:, :], mstat[:, :], mstat[:, :],
              reads=["mstat"], writes=["mr"])
            E("vector", "tensor_scalar", rstat[:, :], rstat[:, :], 1.0 / DM, None, ALU.mult,
              reads=["rstat"], writes=["rstat"])
            E("vector", "tensor_sub", rstat[:, :], rstat[:, :], mr[:, :],
              reads=["rstat", "mr"], writes=["rstat"])
            E("scalar", "activation", rstat[:, :], rstat[:, :], AF.Ln, bias=cpool[:, 0:1],
              scale=1.0, reads=["rstat", "cpool"], writes=["rstat"])
            E("scalar", "activation", rstat[:, :], rstat[:, :], AF.Exp, bias=0.0, scale=-0.5,
              reads=["rstat"], writes=["rstat"])
            E("vector", "scalar_tensor_tensor", mr[:, :], mstat[:, :], -1.0, rstat[:, :],
              ALU.mult, ALU.mult, reads=["mstat", "rstat"], writes=["mr"])
            yield
            for tb in range(TB):
                xb = f"xt{tb % 2}"
                xb_t = xt[tb % 2]
                dma("sync", xb_t[:], x_d[s, tb * 128:(tb + 1) * 128, :], reads=["dx"], writes=[xb])
                xnb = "xnb0" if tb % 2 == 0 else "xnb1"
                E("vector", "tensor_scalar", T[xnb][:], xb_t[:], mstat[:, tb:tb + 1],
                  rstat[:, tb:tb + 1], ALU.subtract, ALU.mult,
                  reads=[xb, "mstat", "rstat"], writes=[xnb])
                yield
                for c2 in range(CK // 4):
                    pw = (tb * 2 + c2) % 2
                    for cc in range(4):
                        c = c2 * 4 + cc
                        E("tensor", "matmul", ps_w[pw][:, cc * 128:(cc + 1) * 128],
                          T[xnb][:, c * 128:(c + 1) * 128], ident[:], start=True, stop=True,
                          reads=[xnb, "consts"], writes=[KW[pw]])
                    eng2 = "scalar" if pw == 0 else "vector"
                    meth = "copy" if eng2 == "scalar" else "tensor_copy"
                    for cc in range(4):
                        c = c2 * 4 + cc
                        E(eng2, meth, xn_fm[c][:, tb * 128:(tb + 1) * 128],
                          ps_w[pw][:, cc * 128:(cc + 1) * 128],
                          reads=[KW[pw]], writes=[f"xnfm{c}g{tb * 128 // MMN}"])
                yield

        def gen_projembed(j, midyield=True):
            B = j % 2
            hA = 2 * j
            for wname in ("wq", "wk", "wv"):
                dma("sync", T[f"{wname}t{B}"][:], wT_s[wname][j],
                    reads=[f"{wname}Ts0", f"{wname}Ts1"], writes=[f"{wname}t{B}"])
            yield
            for wname, magk in (("wq", "qmag"), ("wk", "kmag")):
                for ch in range(NCH):
                    pw = ch % 2
                    for c in range(CK):
                        E("tensor", "matmul", ps_w[pw][:, 0:MMN],
                          T[f"{wname}t{B}"][:, c * 128:(c + 1) * 128],
                          xn_fm[c][:, ch * MMN:(ch + 1) * MMN],
                          start=(c == 0), stop=(c == CK - 1),
                          reads=[f"{wname}t{B}"] + xnk(ch * MMN, MMN), writes=[KW[pw]])
                        if c == 3 and midyield:
                            yield
                    E("scalar", "copy", T[magk][:, ch * MMN:(ch + 1) * MMN],
                      ps_w[pw][:, 0:MMN], reads=[KW[pw]], writes=[magk])
                    yield
                SPW = min(S, 1024)
                for o_ in range(0, S, SPW):
                    E("scalar", "activation", T["tmpe"][:, 0:SPW], T[magk][:, o_:o_ + SPW],
                      AF.Exp, bias=0.0, scale=1.0, reads=[magk], writes=["tmpe"])
                    E("scalar", "activation", T[magk][:, o_:o_ + SPW], T["tmpe"][:, 0:SPW],
                      AF.Ln, bias=cpool[:, 3:4], scale=1.0, reads=["tmpe", "cpool"], writes=[magk])
                    yield
            for ch in range(NCH):
                pw = ch % 2
                for c in range(CK):
                    E("tensor", "matmul", ps_w[pw][:, 0:MMN],
                      T[f"wvt{B}"][:, c * 128:(c + 1) * 128],
                      xn_fm[c][:, ch * MMN:(ch + 1) * MMN],
                      start=(c == 0), stop=(c == CK - 1),
                      reads=[f"wvt{B}"] + xnk(ch * MMN, MMN), writes=[KW[pw]])
                    if c == 3 and midyield:
                        yield
                tv = f"tmpv{pw}"
                E("vector", "tensor_copy", T[tv][:, 0:MMN], ps_w[pw][:, 0:MMN],
                  reads=[KW[pw]], writes=[tv])
                yield
                if True:
                    pass
                for bb in range(MMN // 128):
                    blk = ch * (MMN // 128) + bb
                    pv2 = (ch + 1) % 2
                    E("tensor", "matmul", ps_w[pv2][:, (bb % 4) * 128:(bb % 4 + 1) * 128],
                      T[tv][:, bb * 128:(bb + 1) * 128], ident[:], start=True, stop=True,
                      reads=[tv, "consts"], writes=[KW[pv2]])
                    E("vector", "tensor_copy", T[f"vp{B}"][:, blk * 130:blk * 130 + 64],
                      ps_w[pv2][:, (bb % 4) * 128:(bb % 4) * 128 + 64],
                      reads=[KW[pv2]], writes=[f"vp{B}"])
                    E("vector", "tensor_copy", T[f"vp{B}"][:, blk * 130 + 65:blk * 130 + 129],
                      ps_w[pv2][:, (bb % 4) * 128 + 64:(bb % 4) * 128 + 128],
                      reads=[KW[pv2]], writes=[f"vp{B}"])
                yield
            vpr = T[f"vp{B}"].ap().rearrange("p (b c) -> p b c", c=130)
            E("vector", "memset", vpr[:, :, 64:65], 1.0, reads=[f"vp{B}"], writes=[f"vp{B}"])
            E("vector", "memset", vpr[:, :, 129:130], 1.0, reads=[f"vp{B}"], writes=[f"vp{B}"])
            yield
            for hh in range(2):
                h = 2 * j + hh
                kek = f"ke{B}_{hh}"
                ket = T[kek]
                qhk = f"qh{B}_{hh}"
                qht = T[qhk]
                if hh == 0:
                    E("vector", "tensor_mul", ket[0:64, :], T["kmag"][0:64, :], T1[0:64, :],
                      reads=["kmag", "consts"], writes=[kek])
                    dma("sync", ket[64:128, :], T["kmag"][0:64, :], reads=["kmag"], writes=[kek])
                    E("vector", "tensor_mul", ket[64:128, :], ket[64:128, :], T1[64:128, :],
                      reads=[kek, "consts"], writes=[kek])
                    yield
                    E("vector", "tensor_mul", T["tmp1"][0:64, :], T["qmag"][0:64, :], T1[0:64, :],
                      reads=["qmag", "consts"], writes=["tmp1"])
                    E("vector", "tensor_mul", qht[0:64, :], T["qmag"][0:64, :], T1s[0:64, :],
                      reads=["qmag", "consts"], writes=[qhk])
                    dma("sync", qht[64:128, :], T["qmag"][0:64, :], reads=["qmag"], writes=[qhk])
                    dma("sync", T["tmp1"][64:128, :], T["qmag"][0:64, :], reads=["qmag"], writes=["tmp1"])
                    yield
                    E("vector", "tensor_mul", T["tmp1"][64:128, :], T["tmp1"][64:128, :], T1[64:128, :],
                      reads=["tmp1", "consts"], writes=["tmp1"])
                    E("vector", "tensor_mul", qht[64:128, :], qht[64:128, :], T1s[64:128, :],
                      reads=[qhk, "consts"], writes=[qhk])
                    yield
                else:
                    dma("sync", ket[0:64, :], T["kmag"][64:128, :], reads=["kmag"], writes=[kek])
                    E("vector", "tensor_mul", ket[0:64, :], ket[0:64, :], T1[0:64, :],
                      reads=[kek, "consts"], writes=[kek])
                    E("vector", "tensor_mul", ket[64:128, :], T["kmag"][64:128, :], T1[64:128, :],
                      reads=["kmag", "consts"], writes=[kek])
                    yield
                    dma("sync", qht[0:64, :], T["qmag"][64:128, :], reads=["qmag"], writes=[qhk])
                    dma("sync", T["tmp1"][0:64, :], T["qmag"][64:128, :], reads=["qmag"], writes=["tmp1"])
                    E("vector", "tensor_mul", T["tmp1"][0:64, :], T["tmp1"][0:64, :], T1[0:64, :],
                      reads=["tmp1", "consts"], writes=["tmp1"])
                    E("vector", "tensor_mul", qht[0:64, :], qht[0:64, :], T1s[0:64, :],
                      reads=[qhk, "consts"], writes=[qhk])
                    yield
                    E("vector", "tensor_mul", T["tmp1"][64:128, :], T["qmag"][64:128, :], T1[64:128, :],
                      reads=["qmag", "consts"], writes=["tmp1"])
                    E("vector", "tensor_mul", qht[64:128, :], T["qmag"][64:128, :], T1s[64:128, :],
                      reads=["qmag", "consts"], writes=[qhk])
                    yield
                E("vector", "tensor_scalar", T["tmp1"][:, :], T["tmp1"][:, :], cbdup[:, h:h + 1],
                  None, ALU.mult, reads=["tmp1", "cbdup"], writes=["tmp1"])
                E("vector", "scalar_tensor_tensor", qht[:, :], qht[:, :], sbdup[:, h:h + 1],
                  T["tmp1"][:, :], ALU.mult, ALU.add, reads=[qhk, "tmp1", "sbdup"], writes=[qhk])
                yield
            yield

        if s == 0:
            if S * 2 >= 4 * DM:
                gl, gp = gen_ln(), gen_prep(True)
                ln_done = False
                while not ln_done:
                    for _ in range(2):
                        if not ln_done:
                            ln_done = next(gl, _SENT) is _SENT
                    next(gp, None)
                drive((gp, 4), (gen_projembed(0, midyield=False), 1))
            else:
                for _ in gen_prep(False):
                    pass
                for _ in gen_ln():
                    pass
        else:
            for _ in gen_ln():
                pass

        for c in range(CK):
            E("vector", "reduce_sum", gcv[:, c:c + 1], xn_fm[c][:, :], axis=AX.X,
              reads=xnk(0, S), writes=["gcv"])

        def emit_scores(jp, hh, sq, blk, pp):
            B = jp % 2
            qhk, kek = f"qh{B}_{hh}", f"ke{B}_{hh}"
            for m in range(SQC // MMN):
                E("tensor", "matmul", ps_s[pp][:, m * MMN:(m + 1) * MMN],
                  T[kek][:, blk * 128:(blk + 1) * 128],
                  T[qhk][:, sq * SQC + m * MMN: sq * SQC + (m + 1) * MMN],
                  start=True, stop=True, reads=[kek, qhk], writes=[KS[pp]])

        def emit_exp(pp, ptb):
            E("scalar", "activation", T[f"pt{ptb}"][:, 0:SQC], ps_s[pp][:, 0:SQC],
              AF.Exp, bias=cpool[:, 4:5], scale=0.125,
              reads=[KS[pp], "cpool"], writes=[f"pt{ptb}"])

        def emit_pv(jp, hh, sq, blk, ptb):
            B = jp % 2
            for m in range(SQC // MMN):
                E("tensor", "matmul", ps_pv[:, m * MMN:(m + 1) * MMN],
                  T[f"vp{B}"][:, blk * 130 + hh * 65: blk * 130 + hh * 65 + 65],
                  T[f"pt{ptb}"][:, m * MMN:(m + 1) * MMN],
                  start=(blk == 0), stop=(blk == SKB - 1),
                  reads=[f"vp{B}", f"pt{ptb}"], writes=["ps_pv"])

        def emit_evac(jp, hh, sq):
            h = 2 * jp + hh
            rsi = (h * NSQ + sq) % 2
            if rsi == 0:
                rs_ap, rsk = rowstg0[64:65, 0:SQC], "rowstg0"
            else:
                rs_ap, rsk = xsq[64:65, 0:SQC], "xsq"
            E("vector", "tensor_copy", rs_ap, ps_pv[64:65, 0:SQC],
              reads=["ps_pv"], writes=[rsk])
            dma("sync", rowsums[h:h + 1, sq * SQC:(sq + 1) * SQC],
                rs_ap, reads=[rsk], writes=["rowsums"])
            if hh == 0:
                E("vector", "tensor_copy", attn_in[jp][0:64, sq * SQC:(sq + 1) * SQC],
                  ps_pv[0:64, 0:SQC], reads=["ps_pv"], writes=[f"attnin{jp}"])
            else:
                tBk = f"tmpB{sq % 2}"
                tBt = tmpB[sq % 2]
                E("vector", "tensor_copy", tBt[:, 0:SQC], ps_pv[0:64, 0:SQC],
                  reads=["ps_pv"], writes=[tBk])
                dma("sync", attn_in[jp][64:128, sq * SQC:(sq + 1) * SQC], tBt[:, 0:SQC],
                    reads=[tBk], writes=[f"attnin{jp}"])

        att_pend = []

        def gen_attention(j):
            steps = [(hh, sq, blk) for hh in range(2) for sq in range(NSQ)
                     for blk in range(SKB)]
            nsp = len(steps)
            for i, (hh, sq, blk) in enumerate(steps):
                gi = j * nsp + i
                pp = gi % 2
                ptb = gi % 3
                emit_scores(j, hh, sq, blk, pp)
                emit_exp(pp, ptb)
                if len(att_pend) == 2:
                    oj, ohh, osq, oblk, optb = att_pend.pop(0)
                    emit_pv(oj, ohh, osq, oblk, optb)
                    if oblk == SKB - 1:
                        emit_evac(oj, ohh, osq)
                att_pend.append((j, hh, sq, blk, ptb))
                yield
            if j == NP - 1:
                while att_pend:
                    oj, ohh, osq, oblk, optb = att_pend.pop(0)
                    emit_pv(oj, ohh, osq, oblk, optb)
                    if oblk == SKB - 1:
                        emit_evac(oj, ohh, osq)

        if not (s == 0 and S * 2 >= 4 * DM):
            for _ in gen_projembed(0):
                pass
        for j in range(NP):
            ga = gen_attention(j)
            gp = gen_projembed(j + 1) if j + 1 < NP else iter(())
            drive((ga, 1), (gp, 1))
            if dbg and j == 0 and s == 0:
                for hh in range(2):
                    dma("sync", dbg_qh[hh], T[f"qh0_{hh}"][:], reads=[f"qh0_{hh}"], writes=["dbgqh"])
                    dma("sync", dbg_ke[hh], T[f"ke0_{hh}"][:], reads=[f"ke0_{hh}"], writes=["dbgke"])
                dma("sync", dbg_vp, T["vp0"][:], reads=["vp0"], writes=["dbgvp"])

        def gen_norm():
            E("vector", "reciprocal", rowsums[:, :], rowsums[:, :],
              reads=["rowsums"], writes=["rowsums"])
            for h in range(NH):
                j, hh = h // 2, h % 2
                for sq in range(NSQ):
                    pp = (h * NSQ + sq) % 2
                    lo = hh * 64
                    if pp == 0:
                        rc_ap, rck = rcps[0:1, 0:SQC], "rcps"
                    else:
                        rc_ap, rck = rcpsB[0:1, 0:SQC], "rcpsB"
                    dma("sync", rc_ap, rowsums[h:h + 1, sq * SQC:(sq + 1) * SQC],
                        reads=["rowsums"], writes=[rck, "pt0", "pt1"] if pp else [rck])
                    rrd = [rck, "pt0", "pt1", "consts"] if pp else [rck, "consts"]
                    for m in range(SQC // MMN):
                        E("tensor", "matmul", ps_s[pp][lo:lo + 64, m * MMN:(m + 1) * MMN],
                          onesf[0:1, 0:64], rc_ap[0:1, m * MMN:(m + 1) * MMN],
                          start=True, stop=True, reads=rrd, writes=[KS[pp]])
                    E("vector", "tensor_mul", attn_in[j][lo:lo + 64, sq * SQC:(sq + 1) * SQC],
                      attn_in[j][lo:lo + 64, sq * SQC:(sq + 1) * SQC], ps_s[pp][lo:lo + 64, 0:SQC],
                      reads=[f"attnin{j}", KS[pp]], writes=[f"attnin{j}"])
                    yield

        if dbg and s == 0:
            for c in range(CK):
                dma("sync", dbg_xn[c], xn_fm[c][:], reads=xnk(0, S), writes=["dbgxn"])
                dma("sync", dbg_attn[c], attn_in[c][:], reads=[f"attnin{c}"], writes=["dbgattn"])
            dma("sync", dbg_rows, rowsums[:], reads=["rowsums"], writes=["dbgrows"])

        def gen_gc():
            for k in range(CK):
                xg = xt[k % 2]
                dma("sync", xg[:, 0:DM // 2], g1_d[k * 128:(k + 1) * 128, :],
                    reads=["dg1"], writes=[f"xt{k % 2}"])
                E("tensor", "matmul", ps_w[0][0:1, 0:DM // 2], gcv[:, k:k + 1],
                  xg[:, 0:DM // 2], start=(k == 0), stop=(k == CK - 1),
                  reads=[f"xt{k % 2}", "gcv"], writes=[KW[0]])
                yield
            E("vector", "tensor_copy", xt[1][0:1, 0:DM // 2], ps_w[0][0:1, 0:DM // 2],
              reads=[KW[0]], writes=["xt1"])
            for ob in range(4):
                E("tensor", "matmul", ps_w[1][:, ob:ob + 1],
                  xt[1][0:1, ob * 128:(ob + 1) * 128], onesf[0:1, 0:1],
                  start=True, stop=True, reads=["xt1", "consts"], writes=[KW[1]])
            for ob in range(4):
                E("scalar", "activation", hg1[:, ob:ob + 1], ps_w[1][:, ob:ob + 1], AF.Gelu,
                  bias=gbf1[:, ob:ob + 1], scale=1.0 / S, reads=[KW[1], "consts"], writes=["hg1"])
            yield
            for ob in range(2):
                for k in range(4):
                    E("tensor", "matmul", ps_w[1][:, ob:ob + 1], gw2t[k][:, ob * 128:(ob + 1) * 128],
                      hg1[:, k:k + 1], start=(k == 0), stop=(k == 3),
                      reads=[f"gw2t{k}", "hg1"], writes=[KW[1]])
            for ob in range(2):
                E("scalar", "activation", hg2[:, ob:ob + 1], ps_w[1][:, ob:ob + 1], AF.Gelu,
                  bias=gbf2[:, ob:ob + 1], scale=1.0, reads=[KW[1], "consts"], writes=["hg2"])
            yield
            for k in range(2):
                E("tensor", "matmul", ps_w[0][0:1, 4:5], gw3t[k][:, 0:1], hg2[:, k:k + 1],
                  start=(k == 0), stop=(k == 1), reads=[f"gw3t{k}", "hg2"], writes=[KW[0]])
            E("scalar", "activation", smallt[0:1, 3:4], ps_w[0][0:1, 4:5], AF.Tanh,
              bias=smallt[0:1, 1:2], scale=0.5, reads=[KW[0], "smallt"], writes=["smallt2"])
            yield
            E("vector", "tensor_mul", smallt[0:1, 4:5], smallt[0:1, 2:3], smallt[0:1, 3:4],
              reads=["smallt", "smallt2"], writes=["smallt3"])
            E("vector", "tensor_scalar", smallt[0:1, 4:5], smallt[0:1, 4:5], 0.7, 0.1, ALU.mult, ALU.add,
              reads=["smallt3"], writes=["smallt3"])
            E("vector", "tensor_scalar", smallt[0:1, 5:6], smallt[0:1, 2:3], 0.3, None, ALU.mult,
              reads=["smallt"], writes=["smallt3"])
            yield

        drive((gen_norm(), 6), (gen_gc(), 1))

        if s == 0:
            for c in range(CK):
                dma("sync", woTt[c][:], wT_s["wo"][c],
                    reads=["woTs0", "woTs1"], writes=[f"woTtp{c}"])

        def gen_wo(q):
            for tq in range(QTB):
                tb = q * QTB + tq
                pp = tb % 2
                xb = f"xt{tb % 2}"
                xb_t = xt[tb % 2]
                dma("sync", xb_t[:], x_d[s, tb * 128:(tb + 1) * 128, :], reads=["dx"], writes=[xb])
                for half in range(DM // 512):
                    for c in range(CK):
                        E("tensor", "matmul", ps_s[pp][:, half * 512:(half + 1) * 512],
                          attn_in[c][:, tb * 128:(tb + 1) * 128],
                          woTt[c][:, half * 512:(half + 1) * 512],
                          start=(c == 0), stop=(c == CK - 1),
                          reads=[f"attnin{c}", f"woTtp{c}"], writes=[KS[pp]])
                    yield
                E("vector", "tensor_add", xb_t[:], ps_s[pp][:, 0:DM], xb_t[:],
                  reads=[KS[pp], xb], writes=[xb])
                dma("sync", ffn_s[s, tb * 128:(tb + 1) * 128, :], xb_t[:],
                    reads=[xb], writes=[f"partD{tb % 2}"], no_waw=True)
                yield

        P.alias_barrier(VIEW_PROJ, VIEW_FFN)
        for k in range(CK):
            cast_load(T[f"w1t_{k}"][:], f"w1t_{k}", w1_d[k * 128:(k + 1) * 128, :], "dw1", DM, k % 2)
            cast_load(T[f"w2t_{k}"][:, 0:DM // 2], f"w2t_{k}", w2_d[k * 128:(k + 1) * 128, :],
                      "dw2", DM // 2, k % 2)

        def gen_ffn(q):
            qoff = q * QT
            for tq in range(QTB):
                tb = q * QTB + tq
                xb = f"xt{tq % 2}"
                xb_t = xt[tq % 2]
                dma("sync", xb_t[:], x_d[s, tb * 128:(tb + 1) * 128, :], reads=["dx"], writes=[xb])
                E("vector", "tensor_scalar", T["stream"][:, tq * DM:(tq + 1) * DM], xb_t[:],
                  rstat[:, tb:tb + 1], mr[:, tb:tb + 1], ALU.mult, ALU.add,
                  reads=[xb, "rstat", "mr"], writes=[f"stream{tq}"])
                yield
            for it in range(3):
                if it == 0:
                    def mv1(k):
                        return xn_fm[k][:, qoff:qoff + QT], xnk(qoff, QT)[0:1][0]
                    def mv1r(k):
                        return xnk(qoff, QT)
                else:
                    for tq in range(QTB):
                        sb = f"sbf{tq % 2}"
                        E("vector", "tensor_copy", T[sb][:], T["stream"][:, tq * DM:(tq + 1) * DM],
                          reads=[f"stream{tq}"], writes=[sb])
                        for c2 in range(CK // 4):
                            pw = (tq * 2 + c2) % 2
                            for cc in range(4):
                                c = c2 * 4 + cc
                                E("tensor", "matmul", ps_w[pw][:, cc * 128:(cc + 1) * 128],
                                  T[sb][:, c * 128:(c + 1) * 128], ident[:], start=True, stop=True,
                                  reads=[sb, "consts"], writes=[KW[pw]])
                            eng2 = "scalar" if pw == 0 else "vector"
                            meth2 = "copy" if pw == 0 else "tensor_copy"
                            for cc in range(4):
                                c = c2 * 4 + cc
                                E(eng2, meth2, T[f"ofm{c}"][:, tq * 128:(tq + 1) * 128],
                                  ps_w[pw][:, cc * 128:(cc + 1) * 128],
                                  reads=[KW[pw]], writes=[f"ofm{c}"])

                    def mv1(k):
                        return T[f"ofm{k}"][:, 0:QT], f"ofm{k}"
                    def mv1r(k):
                        return [f"ofm{k}"]
                for ob in range(CK):
                    pw = ob % 2
                    for k in range(CK):
                        src, _ = mv1(k)
                        E("tensor", "matmul", ps_w[pw][:, 0:QT],
                          T[f"w1t_{k}"][:, ob * 128:(ob + 1) * 128],
                          src, start=(k == 0), stop=(k == CK - 1),
                          reads=[f"w1t_{k}"] + mv1r(k), writes=[KW[pw]])
                    E("scalar", "activation", T[f"h1_{ob}"][:, 0:QT], ps_w[pw][:, 0:QT], AF.Gelu,
                      bias=b1f[:, ob:ob + 1], scale=1.0, reads=[KW[pw], "consts"], writes=[f"h1_{ob}"])
                    yield
                for ob in range(CK // 2):
                    pw = ob % 2
                    for k in range(CK):
                        E("tensor", "matmul", ps_w[pw][:, 0:QT],
                          T[f"w2t_{k}"][:, ob * 128:(ob + 1) * 128],
                          T[f"h1_{k}"][:, 0:QT], start=(k == 0), stop=(k == CK - 1),
                          reads=[f"w2t_{k}", f"h1_{k}"], writes=[KW[pw]])
                    E("scalar", "activation", T[f"h2_{ob}"][:, 0:QT], ps_w[pw][:, 0:QT], AF.Gelu,
                      bias=b2f[:, ob:ob + 1], scale=1.0, reads=[KW[pw], "consts"], writes=[f"h2_{ob}"])
                    yield
                for ob in range(CK // 4):
                    pw = ob % 2
                    for k in range(CK // 2):
                        E("tensor", "matmul", ps_w[pw][:, 0:QT],
                          w3t[k][:, ob * 128:(ob + 1) * 128],
                          T[f"h2_{k}"][:, 0:QT], start=(k == 0), stop=(k == CK // 2 - 1),
                          reads=[f"w3t{k}", f"h2_{k}"], writes=[KW[pw]])
                    E("scalar", "activation", T[f"h3_{ob}"][:, 0:QT], ps_w[pw][:, 0:QT], AF.Gelu,
                      bias=b3f[:, ob:ob + 1], scale=1.0, reads=[KW[pw], "consts"], writes=[f"h3_{ob}"])
                    yield
                for k in range(CK // 4):
                    E("tensor", "matmul", ps_w[0][0:1, 0:QT], w4t[k][:, 0:1], T[f"h3_{k}"][:, 0:QT],
                      start=(k == 0), stop=(k == CK // 4 - 1),
                      reads=[f"w4t{k}", f"h3_{k}"], writes=[KW[0]])
                E("scalar", "activation", T["cpt"][0:1, 0:QT], ps_w[0][0:1, 0:QT], AF.Tanh,
                  bias=smallt[0:1, 0:1], scale=0.5, reads=[KW[0], "smallt"], writes=["cpt"])
                E("vector", "tensor_scalar", T["cpt"][0:1, 0:QT], T["cpt"][0:1, 0:QT],
                  smallt[0:1, 5:6], smallt[0:1, 4:5], ALU.mult, ALU.add,
                  reads=["cpt", "smallt3"], writes=["cpt"])
                for tq in range(QTB):
                    E("tensor", "matmul", ps_w[1][:, tq:tq + 1],
                      T["cpt"][0:1, tq * 128:(tq + 1) * 128], onesf[0:1, 0:1],
                      start=True, stop=True, reads=["cpt", "consts"], writes=[KW[1]])
                E("vector", "tensor_copy", T["cptm"][:, 0:QTB], ps_w[1][:, 0:QTB],
                  reads=[KW[1]], writes=["cptm"])
                E("vector", "tensor_scalar", T["ncptm"][:, 0:QTB], T["cptm"][:, 0:QTB], -1.0, None,
                  ALU.mult, reads=["cptm"], writes=["ncptm"])
                for tq in range(QTB):
                    st = T["stream"]
                    o = tq * DM
                    sk2 = f"stream{tq}"
                    t01k = f"t01_{tq % 2}"
                    t01s = T["t01"][:, (tq % 2) * 256:(tq % 2) * 256 + 256]
                    t01s2 = T["t01"][:, (tq % 2) * 256 + 0:(tq % 2) * 256 + 256]
                    cp = T["cptm"][:, tq:tq + 1]
                    ncp = T["ncptm"][:, tq:tq + 1]
                    E("vector", "tensor_copy", T["t01"][:, 0:512], st[:, o:o + 512],
                      reads=[sk2], writes=["t01"]) if False else None
                    E("vector", "scalar_tensor_tensor", st[:, o + 512:o + 768], st[:, o:o + 256],
                      ncp, st[:, o + 512:o + 768], ALU.mult, ALU.add,
                      reads=[sk2, "ncptm"], writes=[sk2]) if False else None
                    E("vector", "tensor_copy", T["t01"][:, 0:512], st[:, o:o + 512],
                      reads=[sk2], writes=["t01"])
                    E("vector", "scalar_tensor_tensor", st[:, o:o + 256], st[:, o + 512:o + 768],
                      cp, st[:, o:o + 256], ALU.mult, ALU.add,
                      reads=[sk2, "cptm"], writes=[sk2])
                    E("vector", "scalar_tensor_tensor", st[:, o + 256:o + 512],
                      st[:, o + 768:o + 1024], ncp, st[:, o + 256:o + 512], ALU.mult, ALU.add,
                      reads=[sk2, "ncptm"], writes=[sk2])
                    E("vector", "scalar_tensor_tensor", st[:, o + 512:o + 768], T["t01"][:, 0:256],
                      ncp, st[:, o + 512:o + 768], ALU.mult, ALU.add,
                      reads=[sk2, "t01", "ncptm"], writes=[sk2])
                    E("vector", "scalar_tensor_tensor", st[:, o + 768:o + 1024],
                      T["t01"][:, 256:512], cp, st[:, o + 768:o + 1024], ALU.mult, ALU.add,
                      reads=[sk2, "t01", "cptm"], writes=[sk2])
                    yield
        def gen_final(q):
            for tq in range(QTB):
                tb = q * QTB + tq
                xb = f"xt{tq % 2}"
                xb_t = xt[tq % 2]
                dma("sync", xb_t[:], ffn_s[s, tb * 128:(tb + 1) * 128, :],
                    reads=["partD0", "partD1"], writes=[xb])
                E("vector", "tensor_add", T["stream"][:, tq * DM:(tq + 1) * DM],
                  T["stream"][:, tq * DM:(tq + 1) * DM], xb_t[:],
                  reads=[f"stream{tq}", xb], writes=[f"stream{tq}"])
                dma("sync", out_d[s, tb * 128:(tb + 1) * 128, :],
                    T["stream"][:, tq * DM:(tq + 1) * DM],
                    reads=[f"stream{tq}"], writes=[f"outD{tq}"], no_waw=True)
                yield

        for q in range(NQ):
            drive((gen_ffn(q), 4), (gen_wo(q), 1))
            for _ in gen_final(q):
                pass

    P.wait_only("sync", reads=[f"outD{i}" for i in range(QTB)] + (["dbgxn", "dbgattn", "dbgrows",
                                          "dbgqh", "dbgke", "dbgvp"] if dbg else []))

    P.finalize()
    with nc.Block() as block:
        P.replay(block)
    return nc


_NC_CACHE = {}


def _get_nc(S, BATCH):
    key = (S, BATCH)
    if key not in _NC_CACHE:
        _NC_CACHE[key] = build_program(S=S, BATCH=BATCH)
    return _NC_CACHE[key]


def kernel(L0_attn_x, L0_attn_wq, L0_attn_wk, L0_attn_wv, L0_attn_wo, L0_attn_pope_bias,
           L0_ffn_cn_w1, L0_ffn_cn_b1, L0_ffn_cn_w2, L0_ffn_cn_b2, L0_ffn_cn_w3, L0_ffn_cn_b3,
           L0_ffn_cn_w4, L0_ffn_cn_b4, L0_ffn_gc_w1, L0_ffn_gc_b1, L0_ffn_gc_w2, L0_ffn_gc_b2,
           L0_ffn_gc_w3, L0_ffn_gc_b3, L0_ffn_adaptive_range):
    x = np.ascontiguousarray(np.asarray(L0_attn_x, dtype=np.float32))
    B_total, S, DM = x.shape
    n_cores = 8
    bpc = B_total // n_cores
    nc = _get_nc(S, bpc)

    def f32(a):
        return np.ascontiguousarray(np.asarray(a, np.float32))

    shared = {
        "wq": f32(L0_attn_wq), "wk": f32(L0_attn_wk), "wv": f32(L0_attn_wv),
        "wo": f32(L0_attn_wo), "pope": f32(L0_attn_pope_bias),
        "cnw1": f32(L0_ffn_cn_w1), "cnb1": f32(L0_ffn_cn_b1),
        "cnw2": f32(L0_ffn_cn_w2), "cnb2": f32(L0_ffn_cn_b2),
        "cnw3": f32(L0_ffn_cn_w3), "cnb3": f32(L0_ffn_cn_b3),
        "cnw4": f32(L0_ffn_cn_w4), "cnb4": f32(L0_ffn_cn_b4),
        "gcw1": f32(L0_ffn_gc_w1), "gcb1": f32(L0_ffn_gc_b1),
        "gcw2": f32(L0_ffn_gc_w2), "gcb2": f32(L0_ffn_gc_b2),
        "gcw3": f32(L0_ffn_gc_w3), "gcb3": f32(L0_ffn_gc_b3),
        "ar": f32(L0_ffn_adaptive_range),
    }
    in_maps = []
    for c in range(n_cores):
        im = dict(shared)
        im["x"] = np.ascontiguousarray(x[c * bpc:(c + 1) * bpc])
        in_maps.append(im)
    res = run_bass_kernel_spmd(nc, in_maps, core_ids=list(range(n_cores)))
    return np.concatenate([r["out"] for r in res.results], axis=0)
